# version 06
# baseline (speedup 1.0000x reference)
import sys
import numpy as np
from contextlib import ExitStack

import concourse.bass as bass
import concourse.mybir as mybir
from concourse.bass_utils import run_bass_kernel_spmd

F32 = mybir.dt.float32
BF16 = mybir.dt.bfloat16
AF = mybir.ActivationFunctionType
ALU = mybir.AluOpType
AX = mybir.AxisListType.X

D = 1024
T = 2048
NT = 16
NF = 8
NB = 2
EPS2 = 2e-6
LNEPS = 1e-5
ACT_SET_LN_EXP = 6



class _Ev:
    __slots__ = ("chain", "idx", "kind", "reads", "writes", "waits", "incs",
                 "label", "issue_ref", "clock", "fire")

    def __init__(self, chain, kind, label=""):
        self.chain = chain
        self.idx = -1
        self.kind = kind
        self.reads = []
        self.writes = []
        self.waits = []
        self.incs = []
        self.label = label
        self.issue_ref = None
        self.clock = None
        self.fire = -1


class Recorder:
    def __init__(self):
        self.chains = {}
        self.sem_owner = {}
        self.tensors = {}
        self.cur_phase = "init"
        self.dma_n = 0
        self.sem_incs = {}

    def chain(self, name):
        return self.chains.setdefault(name, [])

    def add(self, ev):
        ev.label = self.cur_phase + ":" + ev.label
        ch = self.chain(ev.chain)
        ev.idx = len(ch)
        ch.append(ev)
        return ev


def _is_ap(a):
    return hasattr(a, "tensor") and hasattr(a, "offset") and hasattr(a, "ap")


def _foot(rec, ap):
    h = ap.tensor
    name = h.name
    try:
        shape = list(h.shape)
    except Exception:
        shape = None
    dims = [list(d) for d in ap.ap]
    off = ap.offset
    kinds = type(h).__name__ + str(getattr(h, "space", ""))
    if "SB" in kinds or "PSUM" in kinds or "SBUF" in kinds or (
            shape is not None and len(shape) == 2 and shape[0] <= 128 and "DRAM" not in kinds and "DRam" not in kinds):
        F = shape[-1]
        esz = mybir.dt_size(h.dtype) if hasattr(mybir, "dt_size") else (
            2 if h.dtype == BF16 else 4)
        if dims and dims[0][0] == F:
            p0 = off // F
            pcnt = dims[0][1]
            rest = dims[1:]
            base = off % F
        else:
            p0 = off // F
            pcnt = 1
            rest = dims
            base = off % F
        ivals = [(base, base)]
        total = 1
        for stp, cnt in rest[:-1] if rest else []:
            total *= cnt
        if rest:
            inner_stp, inner_cnt = rest[-1]
            outers = rest[:-1]
            combos = [0]
            ok = True
            for stp, cnt in outers:
                if len(combos) * cnt > 4096:
                    ok = False
                    break
                combos = [c + stp * i for c in combos for i in range(cnt)]
            if ok:
                w = abs(inner_stp) * (inner_cnt - 1) + 1
                ivals = sorted((base + c, base + c + w) for c in combos)
            else:
                lo = base
                hi = base
                for stp, cnt in rest:
                    hi += abs(stp) * (cnt - 1)
                ivals = [(lo, hi + 1)]
        else:
            ivals = [(base, base + 1)]
        merged = []
        for lo, hi in ivals:
            if merged and lo <= merged[-1][1]:
                merged[-1] = (merged[-1][0], max(merged[-1][1], hi))
            else:
                merged.append((lo, hi))
        is_psum = "PSUM" in kinds or "Psum" in kinds
        rec.tensors[name] = ("psum" if is_psum else "sbuf", F, esz)
        return (name, (p0, p0 + pcnt), merged, is_psum, esz)
    else:
        lo = off
        hi = off
        for stp, cnt in dims:
            hi += abs(stp) * (cnt - 1)
        rec.tensors[name] = ("dram", None, None)
        return (name, (0, 1), [(lo, hi + 1)], False, 4)


class Shim:
    def __init__(self, rec, instr, ev):
        self._rec = rec
        self._instr = instr
        self._ev = ev

    def then_inc(self, sem, n):
        if self._ev is not None:
            self._ev.incs.append((id(sem), n, getattr(sem, "name", "sem")))
            iss = self._ev.issue_ref if self._ev.kind == "dma_done" else self._ev
            self._rec.sem_incs.setdefault(id(sem), []).append((self._ev, n, iss))
        self._instr.then_inc(sem, n)
        return self


class EngW:
    _DMA = ("dma_start",)

    def __init__(self, rec, eng, chain):
        self._rec = rec
        self._eng = eng
        self._chain = chain

    def wait_ge(self, sem, v):
        ev = _Ev(self._chain, "wait", f"wait {getattr(sem,'name','sem')}>={v}")
        ev.waits.append((id(sem), v, getattr(sem, "name", "sem")))
        self._rec.add(ev)
        self._eng.wait_ge(sem, v)

    def add_instruction(self, ins, **kw):
        ev = _Ev(self._chain, "op", "raw")
        self._rec.add(ev)
        return Shim(self._rec, self._eng.add_instruction(ins, **kw), ev)

    def __getattr__(self, opname):
        eng = self._eng
        fn = getattr(eng, opname)
        rec = self._rec
        chain = self._chain

        def call(*args, **kwargs):
            is_dma = opname in EngW._DMA
            ev = _Ev(chain, "dma_issue" if is_dma else "op", opname)
            aps = []
            for i, a in enumerate(args):
                if _is_ap(a):
                    aps.append((i, a))
            first_w = True
            for i, a in aps:
                f = _foot(rec, a)
                if first_w and i == 0:
                    ev.writes.append(f)
                    first_w = False
                else:
                    ev.reads.append(f)
            for k, a in kwargs.items():
                if _is_ap(a):
                    f = _foot(rec, a)
                    if k in ("accum_out", "out", "out_ap"):
                        ev.writes.append(f)
                    else:
                        ev.reads.append(f)
            if is_dma:
                rec.dma_n += 1
                done = _Ev(f"d{chain}{rec.dma_n}", "dma_done", opname)
                done.reads = ev.reads
                done.writes = ev.writes
                done.issue_ref = ev
                rec.add(ev)
                rec.add(done)
                instr = fn(*args, **kwargs)
                return Shim(rec, instr, done)
            rec.add(ev)
            instr = fn(*args, **kwargs)
            return Shim(rec, instr, ev)

        return call



def build_core(record=False, stop_phase=10**9, dump=None):
    nc = bass.Bass("TRN2", target_bir_lowering=False)
    rec = Recorder() if record else None

    x_h = nc.dram_tensor("x", [NB * T, D], F32, kind="ExternalInput").ap()
    out_h = nc.dram_tensor("out", [NB * T, D], F32, kind="ExternalOutput").ap()
    wnames = ["wq", "wk", "wv", "wo", "fq", "fk", "fv", "fo"]
    w_h = {n: nc.dram_tensor(n, [D, D], F32, kind="ExternalInput").ap() for n in wnames}
    col_h = {n: nc.dram_tensor(n, [128, NF], F32, kind="ExternalInput").ap()
             for n in ["fnw", "fnb", "fqb", "fkb"]}
    row_h = {n: nc.dram_tensor(n, [D], F32, kind="ExternalInput").ap()
             for n in ["fvb", "fob"]}

    es = ExitStack()
    sbuf = lambda name, shape, dt=BF16: es.enter_context(nc.sbuf_tensor(name, shape, dt))
    psum = lambda name, shape, dt=F32: es.enter_context(nc.psum_tensor(name, shape, dt))

    x_res = sbuf("x_res", [128, NT * 1024], F32)
    xnT = sbuf("xnT", [128, NF * T])
    yT = sbuf("yT", [128, NF * T])
    qT = sbuf("qT", [128, 2 * T])
    kT = sbuf("kT", [128, 2 * T])
    vaug = sbuf("vaug", [128, NT * 260])
    asq = sbuf("asq", [128, 2 * 1024])
    ypair = sbuf("ypair", [128, NT * 128])
    absq = ypair
    wtq = sbuf("wtq", [128, NF * 256])
    wtk = sbuf("wtk", [128, NF * 256])
    wtv = sbuf("wtv", [128, NF * 256])
    wto = sbuf("wto", [128, NF * 1024])
    xn2scr = sbuf("xn2scr", [128, 1024], F32)
    sqscr = sbuf("sqscr", [128, 1024], F32)
    xnbuf = [ypair[:, 0:1024], ypair[:, 1024:2048]]
    outscr = [sqscr, xn2scr]
    boundrow = sbuf("boundrow", [65, T])
    kmaxcol = sbuf("kmaxcol", [128, 2])
    stat = sbuf("stat", [128, 64], F32)
    statB = sbuf("statB", [128, 8], F32)
    zbuf = sbuf("zbuf", [128, 16], F32)
    rz = sbuf("rz", [128, 16], F32)
    vsum_sb = sbuf("vsum_sb", [65, 16])
    vsumT_sb = sbuf("vsumT_sb", [16, 65])
    cmask = sbuf("cmask", [128, 128])
    identb = sbuf("identb", [128, 128])
    ones16 = sbuf("ones16", [16, 128])
    negrow = sbuf("negrow", [65, 128])
    seedrow = sbuf("seedrow", [1, 455])
    zerorow = sbuf("zerorow", [1, 388])
    vbrow = sbuf("vbrow", [1, D])
    obrow = sbuf("obrow", [1, D])
    fqbc = sbuf("fqbc", [128, NF], F32)
    fkbc = sbuf("fkbc", [128, NF], F32)
    fnwc = sbuf("fnwc", [128, NF], F32)
    fnbc = sbuf("fnbc", [128, NF], F32)
    ceps = sbuf("ceps", [128, 1], F32)
    spcr = sbuf("spcr", [128, 2], F32)

    ps0 = psum("ps0", [128, 1024])
    ps1 = psum("ps1", [128, 1024])
    pv = [psum(f"pv{i}", [128, 512]) for i in range(3)]
    pt = psum("pt", [128, 1024], BF16)

    sX = es.enter_context(nc.semaphore("sX"))
    sXC = es.enter_context(nc.semaphore("sXC"))
    sRow = es.enter_context(nc.semaphore("sRow"))
    sW = es.enter_context(nc.semaphore("sW"))
    sWO = es.enter_context(nc.semaphore("sWO"))
    sOutP = [es.enter_context(nc.semaphore("sOutP0")),
             es.enter_context(nc.semaphore("sOutP1"))]
    sPE = es.enter_context(nc.semaphore("sPE"))
    sACT = es.enter_context(nc.semaphore("sACT"))
    sDVE = es.enter_context(nc.semaphore("sDVE"))
    sPool = es.enter_context(nc.semaphore("sPool"))

    cnt = {id(s): 0 for s in [sX, sXC, sRow, sW, sWO, sOutP[0], sOutP[1],
                              sPE, sACT, sDVE, sPool]}

    def inc(instr, s, n=1):
        instr.then_inc(s, n)
        cnt[id(s)] += n
        return cnt[id(s)]

    def cur(s):
        return cnt[id(s)]

    def wrap(eng, chain):
        return EngW(rec, eng, chain) if rec is not None else eng

    def tag(name):
        if rec is not None:
            rec.cur_phase = name

    def sp(v):
        v.tensor_copy(spcr[:, 0:1], spcr[:, 1:2])

    def w_slice_src(nm, g_i):
        return bass.AP(w_h[nm].tensor, g_i * 256,
                       [[D, 128], [128 * D, NF], [1, 256]])

    def w_full_src(nm):
        return bass.AP(w_h[nm].tensor, 0,
                       [[D, 128], [128 * D, NF], [1, 1024]])

    wtq_dst = bass.AP(wtq.tensor if hasattr(wtq, "tensor") else wtq, 0,
                      [[NF * 256, 128], [256, NF], [1, 256]])
    wtk_dst = bass.AP(wtk.tensor if hasattr(wtk, "tensor") else wtk, 0,
                      [[NF * 256, 128], [256, NF], [1, 256]])
    wtv_dst = bass.AP(wtv.tensor if hasattr(wtv, "tensor") else wtv, 0,
                      [[NF * 256, 128], [256, NF], [1, 256]])
    wto_dst = bass.AP(wto.tensor if hasattr(wto, "tensor") else wto, 0,
                      [[NF * 1024, 128], [1024, NF], [1, 1024]])

    def wt_group(g, stage, g_i):
        names = ("wq", "wk", "wv") if stage == 0 else ("fq", "fk", "fv")
        for nm, dst in zip(names, (wtq_dst, wtk_dst, wtv_dst)):
            inc(g.dma_start(dst, w_slice_src(nm, g_i)), sW, 16)

    def wt_out(g, stage):
        nm = "wo" if stage == 0 else "fo"
        inc(g.dma_start(wto_dst, w_full_src(nm)), sWO, 16)

    with nc.Block() as blk:
        @blk.sync
        def _(y_):
            y = wrap(y_, "Y")
            src = x_h[0:T, :].rearrange("(t p) f -> p t f", p=128)
            dst = bass.AP(x_res.tensor if hasattr(x_res, "tensor") else x_res,
                          0, [[NT * 1024, 128], [1024, NT], [1, 1024]])
            inc(y.dma_start(dst, src), sX, 16)
            for n, dstc in [("fqb", fqbc), ("fkb", fkbc),
                            ("fnw", fnwc), ("fnb", fnbc)]:
                inc(y.dma_start(dstc[:, :], col_h[n]), sXC, 16)

        @blk.gpsimd
        def _(g_):
            g = wrap(g_, "G")
            inc(g.dma_start(vbrow[0:1, :],
                            row_h["fvb"].rearrange("(a f) -> a f", a=1)), sRow, 16)
            inc(g.dma_start(obrow[0:1, :],
                            row_h["fob"].rearrange("(a f) -> a f", a=1)), sRow, 16)
            wt_group(g, 0, 0)
            wt_out(g, 0)
            g.memset(cmask[:, :], 1.0)
            g.affine_select(out=cmask[:, :], in_=cmask[:, :], compare_op=ALU.is_ge,
                            fill=0.0, base=0, pattern=[[1, 128]], channel_multiplier=-1)
            g.memset(identb[:, :], 0.0)
            g.affine_select(out=identb[:, :], in_=identb[:, :], compare_op=ALU.not_equal,
                            fill=1.0, base=0, pattern=[[-1, 128]], channel_multiplier=1)
            g.memset(ones16[:, :], 1.0)
            g.memset(negrow[:, :], -1.0)
            g.memset(ceps[:, :], LNEPS)
            g.memset(zerorow[:, :], 0.0)
            g.memset(seedrow[:, :], 0.0)
            last = None
            for j7 in range(7):
                last = g.memset(seedrow[0:1, j7 * 65 + 64: j7 * 65 + 65], EPS2)
            inc(last, sPool)

        nc.const_aps.register(F32, LNEPS, ceps[:, 0:1])

        @blk.scalar
        def _(s_):
            s = wrap(s_, "S")
            s.add_instruction(mybir.InstLoadActFuncSet(
                name=nc.get_next_instruction_name(),
                act_func_set_id=ACT_SET_LN_EXP, ins=[], outs=[]))

    def ln_stats(x_target=0, out_target=None):
        with nc.Block() as blk:
            dB = cur(sDVE)

            @blk.vector
            def _(v_):
                v = wrap(v_, "V")
                if x_target:
                    v.wait_ge(sX, x_target)
                if out_target is not None:
                    v.wait_ge(sOutP[0], out_target[0])
                    v.wait_ge(sOutP[1], out_target[1])
                for i in range(NT):
                    xt = x_res[:, i * 1024:(i + 1) * 1024]
                    v.scalar_tensor_tensor(sqscr[:, :], xt, 1.0, xt, ALU.mult,
                                           ALU.mult, accum_out=stat[:, 16 + i:17 + i])
                    v.reduce_sum(stat[:, i:i + 1], xt, axis=AX)
                sp(v)
                v.tensor_scalar(stat[:, 0:16], stat[:, 0:16], 1.0 / D, None, ALU.mult)
                sp(v)
                v.tensor_mul(stat[:, 48:64], stat[:, 0:16], stat[:, 0:16])
                sp(v)
                inc(v.scalar_tensor_tensor(stat[:, 16:32], stat[:, 16:32], 1.0 / D,
                                           stat[:, 48:64], ALU.mult, ALU.subtract), sDVE)

            @blk.scalar
            def _(s_):
                s = wrap(s_, "S")
                s.wait_ge(sDVE, cur(sDVE))
                s.activation(stat[:, 48:64], stat[:, 16:32], AF.Ln, bias=LNEPS, scale=1.0)
                inc(s.activation(stat[:, 32:48], stat[:, 48:64], AF.Exp,
                                 bias=0.0, scale=-0.5), sACT)
            _ = dB

    def ln1_apply(pool_memset_stage=None, pool_wait_pe=0):
        with nc.Block() as blk:
            pB, dB, aB = cur(sPE), cur(sDVE), cur(sACT)

            @blk.tensor
            def _(t_):
                t = wrap(t_, "T")
                t.wait_ge(sPool, 1)
                for i in range(NT):
                    t.wait_ge(sDVE, dB + 2 * i + 1)
                    mm = None
                    for j in range(NF):
                        mm = t.matmul(ps0[:, j * 128:(j + 1) * 128],
                                      ypair[:, (i % 2) * 1024 + j * 128:
                                            (i % 2) * 1024 + (j + 1) * 128],
                                      identb[:, :], start=True, stop=True)
                    inc(mm, sPE)

            @blk.vector
            def _(v_):
                v = wrap(v_, "V")
                v.wait_ge(sACT, aB)
                for i in range(NT):
                    xt = x_res[:, i * 1024:(i + 1) * 1024]
                    inc(v.tensor_scalar(xnbuf[i % 2], xt, stat[:, i:i + 1],
                                        stat[:, 32 + i:33 + i], ALU.subtract,
                                        ALU.mult), sDVE)
                    v.wait_ge(sPE, pB + i + 1)
                    dst = bass.AP(xnT.tensor if hasattr(xnT, "tensor") else xnT,
                                  i * 128, [[NF * T, 128], [T, NF], [1, 128]])
                    inc(v.tensor_copy(dst, ps0[:, :]), sDVE)

            if pool_memset_stage is not None:
                @blk.gpsimd
                def _(g_):
                    g = wrap(g_, "G")
                    g.wait_ge(sPE, pool_wait_pe)
                    last = None
                    if pool_memset_stage == 0:
                        for i in range(NT):
                            for hh in range(4):
                                c0 = i * 260 + hh * 65 + 64
                                last = g.memset(vaug[:, c0:c0 + 1], 1.0)
                    else:
                        for i in range(NT):
                            for j2 in range(2):
                                c0 = i * 260 + j2 * 130 + 128
                                last = g.memset(vaug[:, c0:c0 + 1], 1.0)
                    inc(last, sPool)

    def ln2_apply(pool_wait_pe=0, prefetch=None):
        with nc.Block() as blk:
            pB, dB, aB = cur(sPE), cur(sDVE), cur(sACT)

            @blk.tensor
            def _(t_):
                t = wrap(t_, "T")
                for i in range(NT):
                    t.wait_ge(sDVE, dB + 4 * i + 2)
                    mm = None
                    for j in range(NF):
                        mm = t.matmul(ps0[:, j * 128:(j + 1) * 128],
                                      ypair[:, (i % 2) * 1024 + j * 128:
                                            (i % 2) * 1024 + (j + 1) * 128],
                                      identb[:, :], start=True, stop=True)
                    inc(mm, sPE)

            @blk.scalar
            def _(s_):
                s = wrap(s_, "S")
                for i in range(NT):
                    s.wait_ge(sDVE, dB + 4 * i + 1)
                    s.activation(statB[:, 2:3], statB[:, 1:2], AF.Ln,
                                 bias=LNEPS, scale=1.0)
                    inc(s.activation(statB[:, 3:4], statB[:, 2:3], AF.Exp,
                                     bias=0.0, scale=-0.5), sACT)

            @blk.vector
            def _(v_):
                v = wrap(v_, "V")
                v.wait_ge(sXC, 16 * 4)
                v.wait_ge(sACT, aB)
                for i in range(NT):
                    xt = x_res[:, i * 1024:(i + 1) * 1024]
                    v.tensor_scalar(xn2scr[:, :], xt, stat[:, i:i + 1],
                                    stat[:, 32 + i:33 + i], ALU.subtract, ALU.mult)
                    sp(v)
                    v.scalar_tensor_tensor(sqscr[:, :], xn2scr[:, :], 1.0,
                                           xn2scr[:, :], ALU.mult, ALU.mult,
                                           accum_out=statB[:, 4:5])
                    v.reduce_sum(statB[:, 5:6], xn2scr[:, :], axis=AX)
                    sp(v)
                    v.tensor_scalar(statB[:, 0:1], statB[:, 5:6], 1.0 / D, None, ALU.mult)
                    sp(v)
                    v.tensor_mul(statB[:, 6:7], statB[:, 0:1], statB[:, 0:1])
                    sp(v)
                    inc(v.scalar_tensor_tensor(statB[:, 1:2], statB[:, 4:5], 1.0 / D,
                                               statB[:, 6:7], ALU.mult, ALU.subtract),
                        sDVE)
                    v.wait_ge(sACT, aB + i + 1)
                    inc(v.tensor_scalar(xnbuf[i % 2], xn2scr[:, :], statB[:, 0:1],
                                        statB[:, 3:4], ALU.subtract, ALU.mult), sDVE)
                    inc(v.tensor_add(xt, xt, xn2scr[:, :]), sDVE)
                    v.wait_ge(sPE, pB + i + 1)
                    mm = None
                    for j in range(NF):
                        mm = v.tensor_scalar(xnT[:, j * T + i * 128:j * T + (i + 1) * 128],
                                             ps0[:, j * 128:(j + 1) * 128],
                                             fnwc[:, j:j + 1], fnbc[:, j:j + 1],
                                             ALU.mult, ALU.add)
                    inc(mm, sDVE)

            @blk.gpsimd
            def _(g_):
                g = wrap(g_, "G")
                g.wait_ge(sPE, pool_wait_pe)
                last = None
                for i in range(NT):
                    for j2 in range(2):
                        c0 = i * 260 + j2 * 130 + 128
                        last = g.memset(vaug[:, c0:c0 + 1], 1.0)
                inc(last, sPool)
                if prefetch is not None:
                    prefetch(g)

    def qkv_phase(stage, g_i, wt_target, pool_target, prefetch=None):
        with nc.Block() as blk:
            pB, dB = cur(sPE), cur(sDVE)
            chains = [(proj, d, c) for proj in (0, 1) for d in range(2)
                      for c in range(4)]
            nqk = len(chains)

            def nK(c):
                return 8 if stage == 0 else [8, 4, 2, 1][c]

            @blk.tensor
            def _(t_):
                t = wrap(t_, "T")
                t.wait_ge(sW, wt_target)
                if stage == 1:
                    t.wait_ge(sRow, 32)
                t.wait_ge(sPool, pool_target)
                t.wait_ge(sDVE, dB)
                for idx, (proj, d, c) in enumerate(chains):
                    if idx >= 3:
                        t.wait_ge(sDVE, dB + idx - 2)
                    w = wtq if proj == 0 else wtk
                    dest = pv[idx % 3]
                    mm = None
                    for kk in range(nK(c)):
                        mm = t.matmul(dest[:, 0:512],
                                 w[:, kk * 256 + d * 128: kk * 256 + (d + 1) * 128],
                                 xnT[:, kk * T + c * 512: kk * T + (c + 1) * 512],
                                 start=(kk == 0), stop=(kk == nK(c) - 1))
                    inc(mm, sPE)
                for i in range(NT):
                    idx = nqk + i
                    t.wait_ge(sDVE, dB + idx - 2)
                    dest = pv[idx % 3]
                    c = i // 4
                    if stage == 1:
                        t.matmul(dest[:, 0:256], ones16[0:1, 0:128],
                                 vbrow[0:1, g_i * 256:(g_i + 1) * 256],
                                 start=True, stop=False)
                    mm = None
                    for kk in range(nK(c)):
                        mm = t.matmul(dest[:, 0:256],
                                 xnT[:, kk * T + i * 128: kk * T + (i + 1) * 128],
                                 wtv[:, kk * 256:(kk + 1) * 256],
                                 start=(stage == 0 and kk == 0),
                                 stop=(kk == nK(c) - 1))
                    inc(mm, sPE)

            @blk.vector
            def _(v_):
                v = wrap(v_, "V")
                if stage == 1:
                    v.wait_ge(sXC, 16 * 4)
                for idx, (proj, d, c) in enumerate(chains):
                    v.wait_ge(sPE, pB + idx + 1)
                    src = pv[idx % 3][:, 0:512]
                    dstT = qT if proj == 0 else kT
                    dcol = d * T + c * 512
                    if stage == 0:
                        if proj == 0:
                            mm = v.tensor_scalar(dstT[:, dcol:dcol + 512], src,
                                                 0.125, None, ALU.mult)
                        else:
                            mm = v.tensor_copy(dstT[:, dcol:dcol + 512], src)
                    else:
                        bc = (fqbc if proj == 0 else fkbc)[:, g_i * 2 + d:g_i * 2 + d + 1]
                        mm = v.tensor_scalar(dstT[:, dcol:dcol + 512], src,
                                             bc, None, ALU.add)
                    inc(mm, sDVE)
                for i in range(NT):
                    idx = nqk + i
                    v.wait_ge(sPE, pB + idx + 1)
                    src = pv[idx % 3][:, 0:256]
                    if stage == 0:
                        dst = bass.AP(vaug.tensor if hasattr(vaug, "tensor") else vaug,
                                      i * 260, [[NT * 260, 128], [65, 4], [1, 64]])
                    else:
                        dst = bass.AP(vaug.tensor if hasattr(vaug, "tensor") else vaug,
                                      i * 260, [[NT * 260, 128], [130, 2], [1, 128]])
                    inc(v.tensor_copy(dst, src), sDVE)

            if prefetch is not None:
                @blk.gpsimd
                def _(g_):
                    g = wrap(g_, "G")
                    prefetch(g)

    def att_A(g_i, prefetch=None, prefetch_wait_pe=None):
        with nc.Block() as blk:
            pB, dB, aB = cur(sPE), cur(sDVE), cur(sACT)
            strips = []
            for k in range(NT):
                for s in range(k * 128 // 1024, 2):
                    tq0 = max(k * 128, s * 1024)
                    strips.append((k, s, tq0, (s + 1) * 1024 - tq0, s == k // 8))
            nstr = len(strips)

            heads = []
            pe_c, act_c, dve_c = pB, aB, dB
            for j in range(4):
                h = {"score": {}, "av": {}, "evac": {}, "mask": {}}
                pe_c += 1
                h["seed"] = pe_c
                pe_c += 1
                h["score"][strips[0][0:2]] = pe_c
                for n, (k, s, tq0, w, dg) in enumerate(strips):
                    if n + 1 < nstr:
                        pe_c += 1
                        h["score"][strips[n + 1][0:2]] = pe_c
                    pe_c += 1
                    h["av"][(k, s)] = pe_c
                for (k, s, tq0, w, dg) in strips:
                    act_c += 1
                    h["evac"][(k, s)] = act_c
                    if dg:
                        dve_c += 1
                        h["mask"][k] = dve_c
                dve_c += 1
                h["vsum_evac"] = dve_c
                pe_c += 1
                h["vsumT_tr"] = pe_c
                dve_c += 1
                h["vsumT_evac"] = dve_c
                pe_c += 1
                h["headdone"] = pe_c
                dve_c += 1
                h["division"] = dve_c
                if j % 2 == 1:
                    h["pairtr"] = []
                    for r in range(2):
                        pe_c += 1
                        dve_c += 1
                        h["pairtr"].append((pe_c, dve_c))
                heads.append(h)

            def bank_of(i):
                return (0, i) if i < 7 else ((1, i - 7) if i < 14 else (2, i - 14))

            @blk.tensor
            def _(t_):
                t = wrap(t_, "T")
                scnt = 0
                for j in range(4):
                    h = heads[j]
                    t.wait_ge(sDVE, dB if j == 0 else heads[j - 1]["division"])
                    mm = None
                    for pvx, w in ((pv[0], 455), (pv[1], 455), (pv[2], 130)):
                        mm = t.matmul(pvx[:, 0:w], ones16[0:1, 0:128],
                                      seedrow[0:1, 0:w], start=True, stop=False)
                    inc(mm, sPE)
                    assert cur(sPE) == h["seed"]
                    d, rr = j // 2, (j % 2) * 64

                    def scoreA(n, scnt_n):
                        k, s, tq0, w, dg = strips[n]
                        psP = (ps0, ps1)[scnt_n % 2]
                        if scnt_n >= 2:
                            t.wait_ge(sACT, aB + scnt_n - 1)
                        r0 = tq0 - s * 1024
                        mm = None
                        for c0, c1 in ((r0, 512), (max(r0, 512), 1024)):
                            if c0 < c1:
                                mm = t.matmul(psP[:, c0:c1],
                                         kT[rr:rr + 64,
                                            d * T + k * 128: d * T + (k + 1) * 128],
                                         qT[rr:rr + 64,
                                            d * T + s * 1024 + c0: d * T + s * 1024 + c1],
                                         start=True, stop=True)
                        inc(mm, sPE)
                        assert cur(sPE) == h["score"][(k, s)]

                    scoreA(0, scnt)
                    for n, (k, s, tq0, w, dg) in enumerate(strips):
                        if n + 1 < nstr:
                            scoreA(n + 1, scnt + 1)
                        par = scnt % 2
                        if dg:
                            t.wait_ge(sDVE, h["mask"][k])
                        else:
                            t.wait_ge(sACT, h["evac"][(k, s)])
                        vsl = vaug[:, k * 260 + j * 65: k * 260 + j * 65 + 65]
                        i_lo = max(k, 8 * s)
                        mm = None
                        for i in range(i_lo, 8 * s + 8):
                            bk, off = bank_of(i)
                            mm = t.matmul(pv[bk][:, off * 65: off * 65 + 65],
                                     asq[:, par * 1024 + i * 128 - tq0:
                                         par * 1024 + (i + 1) * 128 - tq0],
                                     vsl, start=False, stop=False)
                        if dg:
                            bk, off = bank_of(k)
                            t.matmul(pv[bk][:, off * 65: off * 65 + 65],
                                     cmask[:, :], vsl, start=False, stop=False)
                            mm = t.matmul(pv[2][0:65, 200 + k: 201 + k], vsl,
                                     cmask[:, 127:128], start=False, stop=False)
                        inc(mm, sPE)
                        assert cur(sPE) == h["av"][(k, s)]
                        scnt += 1
                    t.wait_ge(sDVE, h["vsum_evac"])
                    t.wait_ge(sACT, aB + nstr * (j + 1))
                    inc(t.transpose(pt[0:16, 0:65], vsum_sb[:, :],
                                    identb[0:65, 0:65]), sPE)
                    assert cur(sPE) == h["vsumT_tr"]
                    t.wait_ge(sDVE, h["vsumT_evac"])
                    mm = None
                    for i in range(1, NT):
                        bk, off = bank_of(i)
                        mm = t.matmul(pv[bk][:, off * 65: off * 65 + 65],
                                      ones16[0:i, :], vsumT_sb[0:i, 0:65],
                                      start=False, stop=True)
                    inc(mm, sPE)
                    assert cur(sPE) == h["headdone"]
                    if j % 2 == 1:
                        t.wait_ge(sDVE, h["division"])
                        for r in range(2):
                            if r == 1:
                                t.wait_ge(sDVE, h["pairtr"][0][1])
                            mm = None
                            for i8 in range(8):
                                it = r * 8 + i8
                                mm = t.transpose(pt[:, i8 * 128:(i8 + 1) * 128],
                                                 ypair[:, it * 128:(it + 1) * 128],
                                                 identb[:, :])
                            inc(mm, sPE)
                            assert cur(sPE) == h["pairtr"][r][0]

            @blk.scalar
            def _(s_c):
                s_ = wrap(s_c, "S")
                scnt = 0
                for j in range(4):
                    h = heads[j]
                    for (k, s, tq0, w, dg) in strips:
                        s_.wait_ge(sPE, h["score"][(k, s)])
                        if scnt >= 2:
                            gk, gs = (strips * 4)[scnt - 2][0:2]
                            jj = (scnt - 2) // nstr
                            s_.wait_ge(sPE, heads[jj]["av"][(gk, gs)])
                        par = scnt % 2
                        r0 = tq0 - s * 1024
                        psP = (ps0, ps1)[scnt % 2]
                        mm = s_.activation(asq[:, par * 1024: par * 1024 + w],
                                           psP[:, r0:1024], AF.Square,
                                           bias=1.0, scale=1.0)
                        inc(mm, sACT)
                        assert cur(sACT) == h["evac"][(k, s)]
                        scnt += 1

            @blk.vector
            def _(v_):
                v = wrap(v_, "V")
                scnt = 0
                for j in range(4):
                    h = heads[j]
                    for (k, s, tq0, w, dg) in strips:
                        if dg:
                            v.wait_ge(sACT, h["evac"][(k, s)])
                            par = scnt % 2
                            inc(v.tensor_mul(asq[:, par * 1024: par * 1024 + 128],
                                             asq[:, par * 1024: par * 1024 + 128],
                                             cmask[:, :]), sDVE)
                            assert cur(sDVE) == h["mask"][k]
                        scnt += 1
                    v.wait_ge(sPE, h["av"][(15, 1)])
                    inc(v.tensor_copy(vsum_sb[:, :], pv[2][0:65, 200:216]), sDVE)
                    assert cur(sDVE) == h["vsum_evac"]
                    v.wait_ge(sPE, h["vsumT_tr"])
                    inc(v.tensor_copy(vsumT_sb[:, :], pt[0:16, 0:65]), sDVE)
                    assert cur(sDVE) == h["vsumT_evac"]
                    v.wait_ge(sPE, h["headdone"])
                    v.tensor_copy(zbuf[:, 0:7],
                                  bass.AP(pv[0].tensor if hasattr(pv[0], "tensor") else pv[0],
                                          64, [[512, 128], [65, 7], [1, 1]]))
                    v.tensor_copy(zbuf[:, 7:14],
                                  bass.AP(pv[1].tensor if hasattr(pv[1], "tensor") else pv[1],
                                          64, [[512, 128], [65, 7], [1, 1]]))
                    v.tensor_copy(zbuf[:, 14:16],
                                  bass.AP(pv[2].tensor if hasattr(pv[2], "tensor") else pv[2],
                                          64, [[512, 128], [65, 2], [1, 1]]))
                    sp(v)
                    v.reciprocal(rz[:, :], zbuf[:, :])
                    sp(v)
                    p_off = (j % 2) * 64
                    mm = None
                    for i in range(NT):
                        bk, off = bank_of(i)
                        mm = v.tensor_scalar(ypair[:, i * 128 + p_off: i * 128 + p_off + 64],
                                             pv[bk][:, off * 65: off * 65 + 64],
                                             rz[:, i:i + 1], None, ALU.mult)
                    inc(mm, sDVE)
                    assert cur(sDVE) == h["division"]
                    if j % 2 == 1:
                        pr = g_i * 2 + j // 2
                        for r in range(2):
                            v.wait_ge(sPE, h["pairtr"][r][0])
                            mm = v.tensor_copy(
                                yT[:, pr * T + r * 1024: pr * T + (r + 1) * 1024],
                                pt[:, :])
                            inc(mm, sDVE)
                            assert cur(sDVE) == h["pairtr"][r][1]

            if prefetch is not None:
                @blk.gpsimd
                def _(g_):
                    g = wrap(g_, "G")
                    if prefetch_wait_pe:
                        g.wait_ge(sPE, prefetch_wait_pe)
                    prefetch(g)

    def wo_phase():
        with nc.Block() as blk:
            pB, dB = cur(sPE), cur(sDVE)
            wt_target = cur(sWO)

            @blk.tensor
            def _(t_):
                t = wrap(t_, "T")
                t.wait_ge(sWO, wt_target)
                t.wait_ge(sDVE, dB)
                idx = 0
                for i in range(NT):
                    for ch in range(2):
                        if idx >= 3:
                            t.wait_ge(sDVE, dB + idx - 2)
                        dest = pv[idx % 3]
                        mm = None
                        for j in range(NF):
                            mm = t.matmul(dest[:, 0:512],
                                     yT[:, j * T + i * 128: j * T + (i + 1) * 128],
                                     wto[:, j * 1024 + ch * 512: j * 1024 + (ch + 1) * 512],
                                     start=(j == 0), stop=(j == NF - 1))
                        inc(mm, sPE)
                        idx += 1

            @blk.vector
            def _(v_):
                v = wrap(v_, "V")
                idx = 0
                for i in range(NT):
                    for ch in range(2):
                        v.wait_ge(sPE, pB + idx + 1)
                        xsl = x_res[:, i * 1024 + ch * 512: i * 1024 + (ch + 1) * 512]
                        inc(v.tensor_add(xsl, xsl, pv[idx % 3][:, 0:512]), sDVE)
                        idx += 1

    def bounds_phase(g_i):
        with nc.Block() as blk:
            pB, dB, aB = cur(sPE), cur(sDVE), cur(sACT)

            @blk.tensor
            def _(t_):
                t = wrap(t_, "T")
                t.wait_ge(sACT, aB)
                t.wait_ge(sDVE, dB + 1)
                mm = None
                for c2 in range(2):
                    mm = t.matmul(ps0[0:1, c2 * 512:(c2 + 1) * 512], kmaxcol[:, 0:1],
                             absq[:, c2 * 512:(c2 + 1) * 512], start=True, stop=True)
                inc(mm, sPE)
                for c2 in range(2):
                    mm = t.matmul(ps1[0:1, c2 * 512:(c2 + 1) * 512], kmaxcol[:, 0:1],
                             absq[:, 1024 + c2 * 512: 1024 + (c2 + 1) * 512],
                             start=True, stop=True)
                inc(mm, sPE)
                t.wait_ge(sDVE, dB + 4)
                for c2 in range(2):
                    mm = t.matmul(ps0[0:1, c2 * 512:(c2 + 1) * 512], kmaxcol[:, 1:2],
                             absq[:, c2 * 512:(c2 + 1) * 512], start=True, stop=True)
                inc(mm, sPE)
                for c2 in range(2):
                    mm = t.matmul(ps1[0:1, c2 * 512:(c2 + 1) * 512], kmaxcol[:, 1:2],
                             absq[:, 1024 + c2 * 512: 1024 + (c2 + 1) * 512],
                             start=True, stop=True)
                inc(mm, sPE)

            @blk.vector
            def _(v_):
                v = wrap(v_, "V")
                v.wait_ge(sPE, pB)
                for j in range(2):
                    v.reduce_max(kmaxcol[:, j:j + 1], kT[:, j * T:(j + 1) * T],
                                 axis=AX, apply_absolute_value=True)
                inc(v.scalar_tensor_tensor(absq[:, 0:2048], qT[:, 0:T], -1.0,
                                           qT[:, 0:T], ALU.mult, ALU.max), sDVE)
                v.wait_ge(sPE, pB + 2)
                inc(v.tensor_copy(boundrow[0:1, 0:1024], ps0[0:1, 0:1024]), sDVE)
                inc(v.tensor_copy(boundrow[0:1, 1024:2048], ps1[0:1, 0:1024]), sDVE)
                inc(v.scalar_tensor_tensor(absq[:, 0:2048], qT[:, T:2 * T], -1.0,
                                           qT[:, T:2 * T], ALU.mult, ALU.max), sDVE)
                v.wait_ge(sPE, pB + 4)
                inc(v.tensor_copy(boundrow[64:65, 0:1024], ps0[0:1, 0:1024]), sDVE)
                inc(v.tensor_copy(boundrow[64:65, 1024:2048], ps1[0:1, 0:1024]), sDVE)
            return cur(sDVE)

    def att_B(g_i, bounds_done, prefetch=None, prefetch_wait_pe=None):
        with nc.Block() as blk:
            pB, dB, aB = cur(sPE), cur(sDVE), cur(sACT)
            units = [(j, c2) for j in range(2) for c2 in range(2)]
            plans = []
            pe_c, act_c, dve_c = pB, aB, dB
            trs = {}
            for (j, c2) in units:
                h = {}
                pe_c += 1
                h["seed"] = pe_c
                h["score"] = {}
                h["av"] = {}
                h["evac"] = {}
                pe_c += 1
                h["score"][0] = pe_c
                for k in range(NT):
                    if k + 1 < NT:
                        pe_c += 1
                        h["score"][k + 1] = pe_c
                    pe_c += 1
                    h["av"][k] = pe_c
                for k in range(NT):
                    act_c += 1
                    h["evac"][k] = act_c
                dve_c += 1
                h["division"] = dve_c
                plans.append(h)
                if c2 == 1:
                    lst = []
                    for r in range(2):
                        pe_c += 1
                        dve_c += 1
                        lst.append((pe_c, dve_c))
                    trs[j] = lst

            @blk.tensor
            def _(t_):
                t = wrap(t_, "T")
                for u, (j, c2) in enumerate(units):
                    h = plans[u]
                    t.wait_ge(sDVE, bounds_done if u == 0 else plans[u - 1]["division"])
                    mm = None
                    for pvx, w in ((pv[0], 387), (pv[1], 387), (pv[2], 258)):
                        mm = t.matmul(pvx[:, 0:w], ones16[0:1, 0:128],
                                      zerorow[0:1, 0:w], start=True, stop=False)
                    inc(mm, sPE)
                    assert cur(sPE) == h["seed"]
                    jrow = j * 64

                    def scoreB(k):
                        psP = (ps0, ps1)[k % 2]
                        gidx = u * NT + k
                        if gidx >= 2:
                            t.wait_ge(sACT, aB + gidx - 1)
                        mm = None
                        for half in range(2):
                            t.matmul(psP[:, half * 512:(half + 1) * 512],
                                     negrow[jrow:jrow + 1, 0:128],
                                     boundrow[jrow:jrow + 1,
                                              c2 * 1024 + half * 512:
                                              c2 * 1024 + (half + 1) * 512],
                                     start=True, stop=False)
                            mm = t.matmul(psP[:, half * 512:(half + 1) * 512],
                                     kT[:, j * T + k * 128: j * T + (k + 1) * 128],
                                     qT[:, j * T + c2 * 1024 + half * 512:
                                        j * T + c2 * 1024 + (half + 1) * 512],
                                     start=False, stop=True)
                        inc(mm, sPE)
                        assert cur(sPE) == h["score"][k]

                    scoreB(0)
                    for k in range(NT):
                        if k + 1 < NT:
                            scoreB(k + 1)
                        t.wait_ge(sACT, h["evac"][k])
                        par = k % 2
                        vsl = vaug[:, k * 260 + j * 130: k * 260 + j * 130 + 129]
                        mm = None
                        for i in range(8):
                            bk, off = (0, i) if i < 3 else ((1, i - 3) if i < 6 else (2, i - 6))
                            mm = t.matmul(pv[bk][:, off * 129: off * 129 + 129],
                                          asq[:, par * 1024 + i * 128: par * 1024 + (i + 1) * 128],
                                          vsl, start=False, stop=(k == NT - 1))
                        inc(mm, sPE)
                        assert cur(sPE) == h["av"][k]
                    if c2 == 1:
                        t.wait_ge(sDVE, h["division"])
                        for r in range(2):
                            if r == 1:
                                t.wait_ge(sDVE, trs[j][0][1])
                            mm = None
                            for i8 in range(8):
                                it = r * 8 + i8
                                mm = t.transpose(pt[:, i8 * 128:(i8 + 1) * 128],
                                                 ypair[:, it * 128:(it + 1) * 128],
                                                 identb[:, :])
                            inc(mm, sPE)
                            assert cur(sPE) == trs[j][r][0]

            @blk.scalar
            def _(s_):
                s = wrap(s_, "S")
                for u, (j, c2) in enumerate(units):
                    h = plans[u]
                    for k in range(NT):
                        par = k % 2
                        s.wait_ge(sPE, h["score"][k])
                        if k >= 2:
                            s.wait_ge(sPE, h["av"][k - 2])
                        elif u > 0:
                            s.wait_ge(sPE, plans[u - 1]["av"][NT - 2 + k])
                        mm = s.activation(asq[:, par * 1024: par * 1024 + 1024],
                                          (ps0, ps1)[k % 2][:, 0:1024], AF.Exp,
                                          bias=0.0, scale=1.0 / 32.0)
                        inc(mm, sACT)
                        assert cur(sACT) == h["evac"][k]

            @blk.vector
            def _(v_):
                v = wrap(v_, "V")
                for u, (j, c2) in enumerate(units):
                    h = plans[u]
                    v.wait_ge(sPE, h["av"][NT - 1])
                    v.tensor_copy(zbuf[:, 0:3],
                                  bass.AP(pv[0].tensor if hasattr(pv[0], "tensor") else pv[0],
                                          128, [[512, 128], [129, 3], [1, 1]]))
                    v.tensor_copy(zbuf[:, 3:6],
                                  bass.AP(pv[1].tensor if hasattr(pv[1], "tensor") else pv[1],
                                          128, [[512, 128], [129, 3], [1, 1]]))
                    v.tensor_copy(zbuf[:, 6:8],
                                  bass.AP(pv[2].tensor if hasattr(pv[2], "tensor") else pv[2],
                                          128, [[512, 128], [129, 2], [1, 1]]))
                    sp(v)
                    v.reciprocal(rz[:, 0:8], zbuf[:, 0:8])
                    sp(v)
                    mm = None
                    for i in range(8):
                        it = c2 * 8 + i
                        bk, off = (0, i) if i < 3 else ((1, i - 3) if i < 6 else (2, i - 6))
                        mm = v.tensor_scalar(
                            ypair[:, it * 128:(it + 1) * 128],
                            pv[bk][:, off * 129: off * 129 + 128],
                            rz[:, i:i + 1], None, ALU.mult)
                    inc(mm, sDVE)
                    assert cur(sDVE) == h["division"]
                    if c2 == 1:
                        pr = g_i * 2 + j
                        for r in range(2):
                            v.wait_ge(sPE, trs[j][r][0])
                            mm = v.tensor_copy(
                                yT[:, pr * T + r * 1024: pr * T + (r + 1) * 1024],
                                pt[:, :])
                            inc(mm, sDVE)
                            assert cur(sDVE) == trs[j][r][1]

            if prefetch is not None:
                @blk.gpsimd
                def _(g_):
                    g = wrap(g_, "G")
                    if prefetch_wait_pe:
                        g.wait_ge(sPE, prefetch_wait_pe)
                    prefetch(g)

    def out_phase(b, xload_slot=None):
        echunks = {0: [(0, 512), (512, 1024)], 1: [(0, 512)], 2: [(0, 256)],
                   3: [(0, 128)]}
        with nc.Block() as blk:
            pB, dB = cur(sPE), cur(sDVE)
            oB = [cur(sOutP[0]), cur(sOutP[1])]
            wt_target = cur(sWO)
            tasks = [(e, i) for e in range(4) for i in range(4 * e, 4 * e + 4)]
            nchunks = sum(len(echunks[e]) for e, i in tasks)

            task_of_chunk = []
            for ti, (e, i) in enumerate(tasks):
                for _ in echunks[e]:
                    task_of_chunk.append(ti)

            @blk.tensor
            def _(t_):
                t = wrap(t_, "T")
                t.wait_ge(sWO, wt_target)
                t.wait_ge(sRow, 32)
                t.wait_ge(sDVE, dB)
                idx = 0
                for e, i in tasks:
                    nK = [8, 4, 2, 1][e]
                    for (o0, o1) in echunks[e]:
                        if idx >= 3:
                            t.wait_ge(sDVE, dB + task_of_chunk[idx - 3] + 1)
                        dest = pv[idx % 3]
                        t.matmul(dest[:, 0:o1 - o0], ones16[0:1, 0:128],
                                 obrow[0:1, o0:o1], start=True, stop=False)
                        mm = None
                        for kk in range(nK):
                            mm = t.matmul(dest[:, 0:o1 - o0],
                                     yT[:, kk * T + i * 128: kk * T + (i + 1) * 128],
                                     wto[:, kk * 1024 + o0: kk * 1024 + o1],
                                     start=False, stop=(kk == nK - 1))
                        inc(mm, sPE)
                        idx += 1

            @blk.vector
            def _(v_):
                v = wrap(v_, "V")
                idx = 0
                for ti, (e, i) in enumerate(tasks):
                    par = ti % 2
                    m_e = 1024 >> e
                    if ti >= 2:
                        v.wait_ge(sOutP[par], oB[par] + (ti // 2) * 16)
                    xt = x_res[:, i * 1024:(i + 1) * 1024]
                    last = None
                    for (o0, o1) in echunks[e]:
                        v.wait_ge(sPE, pB + idx + 1)
                        last = v.tensor_add(outscr[par][:, o0:o1], xt[:, o0:o1],
                                            pv[idx % 3][:, 0:o1 - o0])
                        idx += 1
                    if m_e < 1024:
                        last = v.tensor_copy(outscr[par][:, m_e:1024], xt[:, m_e:1024])
                    inc(last, sDVE)

            @blk.sync
            def _(y_):
                y = wrap(y_, "Y")
                for ti, (e, i) in enumerate(tasks):
                    par = ti % 2
                    y.wait_ge(sDVE, dB + ti + 1)
                    inc(y.dma_start(out_h[b * T + i * 128: b * T + (i + 1) * 128, :],
                                    outscr[par][:, :]), sOutP[par], 16)
                if b == 0:
                    y.wait_ge(sDVE, dB + NT)
                    srcx = x_h[T:2 * T, :].rearrange("(t p) f -> p t f", p=128)
                    dstx = bass.AP(x_res.tensor if hasattr(x_res, "tensor") else x_res,
                                   0, [[NT * 1024, 128], [1024, NT], [1, 1024]])
                    xload_slot[0] = inc(y.dma_start(dstx, srcx), sX, 16)
                else:
                    y.wait_ge(sOutP[0], cur(sOutP[0]))
                    y.wait_ge(sOutP[1], cur(sOutP[1]))

            @blk.gpsimd
            def _(g_):
                g = wrap(g_, "G")
                if b == 0:
                    g.wait_ge(sPE, pB + nchunks)
                    wt_out(g, 0)

    phase_n = [0]

    def ph():
        phase_n[0] += 1
        return phase_n[0] <= stop_phase

    xload_slot = [16]
    out_after_b0 = None
    for b in range(NB):
        xt_target = xload_slot[0]
        if ph():
            tag(f"b{b}ln_stats1")
            ln_stats(x_target=xt_target, out_target=(None if b == 0 else out_after_b0))
        if ph():
            tag(f"b{b}ln1")
            ln1_apply(pool_memset_stage=0, pool_wait_pe=cur(sPE))
        poolA = cur(sPool)
        for g_i in range(4):
            if ph():
                tag(f"b{b}qkvA{g_i}")
                qkv_phase(0, g_i, wt_target=cur(sW), pool_target=poolA)
            if ph():
                tag(f"b{b}attA{g_i}")
                if g_i < 3:
                    att_A(g_i, prefetch=(lambda g, gn=g_i: wt_group(g, 0, gn + 1)),
                          prefetch_wait_pe=cur(sPE))
                else:
                    att_A(g_i, prefetch=(lambda g: wt_group(g, 1, 0)),
                          prefetch_wait_pe=cur(sPE))
        if ph():
            tag(f"b{b}wo")
            wo_phase()
        if ph():
            tag(f"b{b}ln_stats2")
            ln_stats()
        if ph():
            tag(f"b{b}ln2")
            ln2_apply(pool_wait_pe=cur(sPE),
                      prefetch=(lambda g: wt_out(g, 1)))
        poolB = cur(sPool)
        for g_i in range(4):
            if ph():
                tag(f"b{b}qkvB{g_i}")
                qkv_phase(1, g_i, wt_target=cur(sW), pool_target=poolB)
            if ph():
                tag(f"b{b}bounds{g_i}")
                bdone = bounds_phase(g_i)
            if ph():
                tag(f"b{b}attB{g_i}")
                if g_i < 3:
                    att_B(g_i, bdone, prefetch=(lambda g, gn=g_i: wt_group(g, 1, gn + 1)),
                          prefetch_wait_pe=cur(sPE))
                elif b == 0:
                    att_B(g_i, bdone, prefetch=(lambda g: wt_group(g, 0, 0)),
                          prefetch_wait_pe=cur(sPE))
                else:
                    att_B(g_i, bdone)
        if ph():
            tag(f"b{b}out")
            out_phase(b, xload_slot)
        if b == 0:
            out_after_b0 = (cur(sOutP[0]), cur(sOutP[1]))

    with nc.Block() as blk:
        if phase_n[0] > stop_phase:
            @blk.tensor
            def _(t_):
                t = wrap(t_, "T")
                t.wait_ge(sDVE, cur(sDVE))
                t.wait_ge(sACT, cur(sACT))
            @blk.vector
            def _(v_):
                v = wrap(v_, "V")
                v.wait_ge(sPE, cur(sPE))
                v.wait_ge(sPool, cur(sPool))
            @blk.scalar
            def _(s_):
                s = wrap(s_, "S")
                s.wait_ge(sPE, cur(sPE))
            @blk.gpsimd
            def _(g_):
                g = wrap(g_, "G")
                g.wait_ge(sPE, cur(sPE))
            dump_map = {"x_res": (x_res, NT * 1024), "xnT": (xnT, NF * T),
                        "yT": (yT, NF * T), "qT": (qT, 2 * T), "kT": (kT, 2 * T),
                        "vaug": (vaug, NT * 260), "asq": (asq, 2048),
                        "ypair": (ypair, NT * 128), "boundrow": (boundrow, T),
                        "stat": (stat, 64), "wtq": (wtq, NF * 256),
                        "wtv": (wtv, NF * 256), "wto": (wto, NF * 1024)}
            dt_, dw_ = dump_map[dump] if dump else (x_res, NT * 1024)

            @blk.gpsimd
            def _(g_):
                g = wrap(g_, "G")
                g.wait_ge(sX, cur(sX))
                g.wait_ge(sXC, cur(sXC))
                g.wait_ge(sRow, cur(sRow))
                g.wait_ge(sW, cur(sW))
                g.wait_ge(sWO, cur(sWO))
                g.wait_ge(sPE, cur(sPE))
                g.wait_ge(sDVE, cur(sDVE))
                g.wait_ge(sACT, cur(sACT))
                nt_ = (dw_ + 1023) // 1024
                for i in range(NB * NT):
                    ii = i % nt_
                    w = min(1024, dw_ - ii * 1024)
                    inc(g.dma_start(out_h[i * 128:(i + 1) * 128, 0:w],
                                    dt_[:, ii * 1024: ii * 1024 + w]),
                        sOutP[i % 2], 16)
                g.wait_ge(sOutP[0], cur(sOutP[0]))
                g.wait_ge(sOutP[1], cur(sOutP[1]))
        else:
            @blk.sync
            def _(y_):
                y = wrap(y_, "Y")
                y.wait_ge(sOutP[0], cur(sOutP[0]))
                y.wait_ge(sOutP[1], cur(sOutP[1]))

    es.close()
    return (nc, rec) if record else nc



def verify(rec):
    chains = sorted(rec.chains.keys())
    cidx = {c: i for i, c in enumerate(chains)}
    nch = len(chains)
    heads = {c: 0 for c in chains}
    semval = {}
    sem_prefix = {}
    fire_order = []

    races = []

    def join(a, b):
        return [max(x, y) for x, y in zip(a, b)]

    prev_clock = {c: [0] * nch for c in chains}

    progressed = True
    total = sum(len(v) for v in rec.chains.values())
    fired = 0
    while fired < total:
        progressed = False
        for c in chains:
            ch = rec.chains[c]
            while heads[c] < len(ch):
                ev = ch[heads[c]]
                if ev.kind == "dma_done" and ev.issue_ref.fire < 0:
                    break
                ok = True
                for sid, v, nm in ev.waits:
                    if semval.get(sid, 0) < v:
                        ok = False
                        break
                if not ok:
                    break
                clk = list(prev_clock[c])
                for sid, v, nm in ev.waits:
                    lst = sem_prefix.get(sid, [])
                    lo, hi = 0, len(lst)
                    while lo < hi:
                        mid = (lo + hi) // 2
                        if lst[mid][0] >= v:
                            hi = mid
                        else:
                            lo = mid + 1
                    if lo < len(lst):
                        clk = join(clk, lst[lo][1])
                    elif lst:
                        clk = join(clk, lst[-1][1])
                clk[cidx[c]] = heads[c] + 1
                ev.clock = clk
                ev.fire = len(fire_order)
                fire_order.append(ev)
                prev_clock[c] = clk
                for sid, n, nm in ev.incs:
                    semval[sid] = semval.get(sid, 0) + n
                    lst = sem_prefix.setdefault(sid, [])
                    pc = join(lst[-1][1], clk) if lst else clk
                    lst.append((semval[sid], pc))
                heads[c] += 1
                fired += 1
                progressed = True
        if not progressed:
            msgs = []
            for c in chains:
                if heads[c] < len(rec.chains[c]):
                    ev = rec.chains[c][heads[c]]
                    msgs.append(f"  chain {c} stuck at #{heads[c]} {ev.label} "
                                f"waits={[(nm, v, semval.get(sid, 0)) for sid, v, nm in ev.waits]}")
            raise RuntimeError("DEADLOCK:\n" + "\n".join(msgs))

    def hb(a, b):
        return b.clock[cidx[a.chain]] >= a.idx + 1

    sem_issues = []
    for c in chains:
        for ev in rec.chains[c]:
            for sid, v, nm in ev.waits:
                incs = rec.sem_incs.get(sid, [])
                if len({ie.chain for ie, n, iss in incs}) <= 1:
                    continue
                run = 0
                k = len(incs)
                for idx2, (ie, n, iss) in enumerate(incs):
                    run += n
                    if run >= v:
                        k = idx2 + 1
                        break
                for ie, n, iss in incs[k:]:
                    if not hb(ev, iss):
                        sem_issues.append((nm, v, ev.chain, ev.idx, ev.label,
                                           iss.chain, iss.idx, iss.label))
    for s in sem_issues[:40]:
        print("SEM-ORDER", s)
    if sem_issues:
        races.append(("SEM-ORDER-COUNT", len(sem_issues), "", 0, "", "", 0, ""))

    state = {}

    def cells_of(foot):
        name, pr, ivals, is_psum, esz = foot
        pgs = set(p // 64 for p in range(pr[0], pr[1], 64)) | {(pr[1] - 1) // 64}
        out = []
        for lo, hi in ivals:
            for cell in range(lo // 128, (hi - 1) // 128 + 1):
                for pg in pgs:
                    out.append(((name, pg, cell), lo, hi))
        return out

    def overlap(iv1, iv2):
        return iv1[0] < iv2[1] and iv2[0] < iv1[1]

    bank_state = {}

    def access(ev, foot, is_write):
        name, pr, ivals, is_psum, esz = foot
        if is_psum:
            banks = set()
            for lo, hi in ivals:
                b0 = (lo * esz) // 2048
                b1 = ((hi - 1) * esz) // 2048
                banks.update(range(b0, b1 + 1))
            for bk in banks:
                st = bank_state.setdefault((name, bk), [None, {}])
                if ev.chain == "T" and is_write:
                    for c, other in st[1].items():
                        if not hb(other, ev):
                            races.append(("PSUM-BANK", (name, bk), other.chain,
                                          other.idx, other.label, ev.chain,
                                          ev.idx, ev.label))
                    st[0] = ev
                elif ev.chain in ("V", "S"):
                    if st[0] is not None and not hb(st[0], ev):
                        races.append(("PSUM-BANK", (name, bk), st[0].chain,
                                      st[0].idx, st[0].label, ev.chain,
                                      ev.idx, ev.label))
                    st[1][ev.chain] = ev
        for key, lo, hi in cells_of(foot):
            st = state.setdefault(key, {})
            for oc, rec2 in st.items():
                if oc == ev.chain:
                    continue
                ow_iv, ow_ev, or_iv, or_ev = rec2
                if is_write:
                    if ow_ev is not None and any(overlap((lo, hi), i) for i in ow_iv):
                        _chk(ow_ev, ev, races, key, "WW")
                    if or_ev is not None and any(overlap((lo, hi), i) for i in or_iv):
                        _chk(or_ev, ev, races, key, "WR->W")
                else:
                    if ow_ev is not None and any(overlap((lo, hi), i) for i in ow_iv):
                        _chk(ow_ev, ev, races, key, "W->R")
            mine = st.setdefault(ev.chain, [[], None, [], None])
            if is_write:
                mine[0].append((lo, hi))
                mine[1] = ev
            else:
                mine[2].append((lo, hi))
                mine[3] = ev

    def _chk(a, b, races, key, kind):
        aa = a
        bb = b
        if a.kind == "dma_done":
            aa = a
        if b.kind == "dma_done" and b.issue_ref is not None:
            bb = b.issue_ref
        if not hb(aa, bb if bb.clock is not None else b):
            races.append((kind, key, a.chain, a.idx, a.label, b.chain, b.idx, b.label))

    for ev in fire_order:
        if ev.kind == "dma_issue":
            continue
        for f in ev.reads:
            access(ev, f, False)
        for f in ev.writes:
            access(ev, f, True)

    def foot_overlap(fa, fb):
        if fa[0] != fb[0]:
            return False
        if fa[1][1] <= fb[1][0] or fb[1][1] <= fa[1][0]:
            return False
        for a in fa[2]:
            for b in fb[2]:
                if a[0] < b[1] and b[0] < a[1]:
                    return True
        return False

    for chain in ("V",):
        cV = cidx[chain]
        evs = [e for e in rec.chains.get(chain, []) if e.kind == "op"]
        other = [c for c in chains if c != chain and not c.startswith("d")]
        for i in range(len(evs) - 1):
            a, b = evs[i], evs[i + 1]
            hit = False
            for fw in a.writes:
                for fr in b.reads + b.writes:
                    if foot_overlap(fw, fr):
                        hit = True
            if not hit:
                continue
            proven = False
            for c in other:
                lst = rec.chains[c]
                lo, hi = 0, len(lst)
                while lo < hi:
                    mid = (lo + hi) // 2
                    if lst[mid].clock is not None and lst[mid].clock[cV] >= a.idx + 1:
                        hi = mid
                    else:
                        lo = mid + 1
                if lo < len(lst) and b.clock[cidx[c]] >= lo + 1:
                    proven = True
                    break
            if not proven:
                for fw in a.writes:
                    for fr in b.reads + b.writes:
                        if foot_overlap(fw, fr):
                            races.append(("DVE-ADJ", fw[0], a.chain, a.idx,
                                          a.label, b.chain, b.idx, b.label))
    return races



_CACHE = {}


def kernel(**inputs):
    e_cnt = np.asarray(inputs["L0_ffn_router_prob"]).shape[2]
    assert e_cnt == 4, f"kernel built for 4 experts, got {e_cnt}"
    x = np.ascontiguousarray(np.asarray(inputs["L0_attn_hidden_states"],
                                        dtype=np.float32))
    wt = {
        "wq": "L0_attn_Wq", "wk": "L0_attn_Wk", "wv": "L0_attn_Wv",
        "wo": "L0_attn_Wo", "fq": "L0_ffn_q_w", "fk": "L0_ffn_k_w",
        "fv": "L0_ffn_v_w", "fo": "L0_ffn_o_w",
    }
    common = {}
    for k, v in wt.items():
        w = np.asarray(inputs[v], dtype=np.float32)
        common[k] = np.ascontiguousarray(w.T)
    for k, v in [("fnw", "L0_ffn_norm_w"), ("fnb", "L0_ffn_norm_b"),
                 ("fqb", "L0_ffn_q_b"), ("fkb", "L0_ffn_k_b")]:
        a = np.asarray(inputs[v], dtype=np.float32)
        common[k] = np.ascontiguousarray(a.reshape(NF, 128).T)
    for k, v in [("fvb", "L0_ffn_v_b"), ("fob", "L0_ffn_o_b")]:
        common[k] = np.ascontiguousarray(np.asarray(inputs[v], dtype=np.float32))
    stop_phase = _CACHE.get("stop_phase", 10**9)
    dump = _CACHE.get("dump", None)
    key = ("nc", stop_phase, dump)
    if key not in _CACHE:
        _CACHE[key] = build_core(stop_phase=stop_phase, dump=dump)
    nc = _CACHE[key]
    in_maps = []
    for c in range(8):
        m = dict(common)
        m["x"] = np.ascontiguousarray(x[2 * c:2 * c + 2].reshape(NB * T, D))
        in_maps.append(m)
    res = run_bass_kernel_spmd(nc, in_maps, core_ids=list(range(8)))
    out = np.concatenate([r["out"].reshape(NB, T, D) for r in res.results], axis=0)
    return out.astype(np.float32)


if __name__ == "__main__":
    if "--check" in sys.argv:
        nc, rec = build_core(record=True)
        print("built; chains:", {c: len(v) for c, v in rec.chains.items()})
        races = verify(rec)
        print(f"verify: {len(races)} potential races")
        for r in races[:60]:
            print("  RACE", r)
    else:
        nc = build_core()
        print("built OK; instructions:", nc.n_instructions())


# version 07
# speedup vs baseline: 1.2200x; 1.2200x over previous
import sys
import numpy as np
from contextlib import ExitStack

import concourse.bass as bass
import concourse.mybir as mybir
from concourse.bass_utils import run_bass_kernel_spmd

F32 = mybir.dt.float32
BF16 = mybir.dt.bfloat16
AF = mybir.ActivationFunctionType
ALU = mybir.AluOpType
AX = mybir.AxisListType.X

D = 1024
T = 2048
NT = 16
NF = 8
NB = 2
EPS2 = 2e-6
LNEPS = 1e-5
ACT_SET_LN_EXP = 6



class _Ev:
    __slots__ = ("chain", "idx", "kind", "reads", "writes", "waits", "incs",
                 "label", "issue_ref", "clock", "fire")

    def __init__(self, chain, kind, label=""):
        self.chain = chain
        self.idx = -1
        self.kind = kind
        self.reads = []
        self.writes = []
        self.waits = []
        self.incs = []
        self.label = label
        self.issue_ref = None
        self.clock = None
        self.fire = -1


class Recorder:
    def __init__(self):
        self.chains = {}
        self.sem_owner = {}
        self.tensors = {}
        self.cur_phase = "init"
        self.dma_n = 0
        self.sem_incs = {}

    def chain(self, name):
        return self.chains.setdefault(name, [])

    def add(self, ev):
        ev.label = self.cur_phase + ":" + ev.label
        ch = self.chain(ev.chain)
        ev.idx = len(ch)
        ch.append(ev)
        return ev


def _is_ap(a):
    return hasattr(a, "tensor") and hasattr(a, "offset") and hasattr(a, "ap")


def _foot(rec, ap):
    h = ap.tensor
    name = h.name
    try:
        shape = list(h.shape)
    except Exception:
        shape = None
    dims = [list(d) for d in ap.ap]
    off = ap.offset
    kinds = type(h).__name__ + str(getattr(h, "space", ""))
    if "SB" in kinds or "PSUM" in kinds or "SBUF" in kinds or (
            shape is not None and len(shape) == 2 and shape[0] <= 128 and "DRAM" not in kinds and "DRam" not in kinds):
        F = shape[-1]
        esz = mybir.dt_size(h.dtype) if hasattr(mybir, "dt_size") else (
            2 if h.dtype == BF16 else 4)
        if dims and dims[0][0] == F:
            p0 = off // F
            pcnt = dims[0][1]
            rest = dims[1:]
            base = off % F
        else:
            p0 = off // F
            pcnt = 1
            rest = dims
            base = off % F
        ivals = [(base, base)]
        total = 1
        for stp, cnt in rest[:-1] if rest else []:
            total *= cnt
        if rest:
            inner_stp, inner_cnt = rest[-1]
            outers = rest[:-1]
            combos = [0]
            ok = True
            for stp, cnt in outers:
                if len(combos) * cnt > 4096:
                    ok = False
                    break
                combos = [c + stp * i for c in combos for i in range(cnt)]
            if ok:
                w = abs(inner_stp) * (inner_cnt - 1) + 1
                ivals = sorted((base + c, base + c + w) for c in combos)
            else:
                lo = base
                hi = base
                for stp, cnt in rest:
                    hi += abs(stp) * (cnt - 1)
                ivals = [(lo, hi + 1)]
        else:
            ivals = [(base, base + 1)]
        merged = []
        for lo, hi in ivals:
            if merged and lo <= merged[-1][1]:
                merged[-1] = (merged[-1][0], max(merged[-1][1], hi))
            else:
                merged.append((lo, hi))
        is_psum = "PSUM" in kinds or "Psum" in kinds
        rec.tensors[name] = ("psum" if is_psum else "sbuf", F, esz)
        return (name, (p0, p0 + pcnt), merged, is_psum, esz)
    else:
        lo = off
        hi = off
        for stp, cnt in dims:
            hi += abs(stp) * (cnt - 1)
        rec.tensors[name] = ("dram", None, None)
        return (name, (0, 1), [(lo, hi + 1)], False, 4)


class Shim:
    def __init__(self, rec, instr, ev):
        self._rec = rec
        self._instr = instr
        self._ev = ev

    def then_inc(self, sem, n):
        if self._ev is not None:
            self._ev.incs.append((id(sem), n, getattr(sem, "name", "sem")))
            iss = self._ev.issue_ref if self._ev.kind == "dma_done" else self._ev
            self._rec.sem_incs.setdefault(id(sem), []).append((self._ev, n, iss))
        self._instr.then_inc(sem, n)
        return self


class EngW:
    _DMA = ("dma_start",)

    def __init__(self, rec, eng, chain):
        self._rec = rec
        self._eng = eng
        self._chain = chain

    def wait_ge(self, sem, v):
        ev = _Ev(self._chain, "wait", f"wait {getattr(sem,'name','sem')}>={v}")
        ev.waits.append((id(sem), v, getattr(sem, "name", "sem")))
        self._rec.add(ev)
        self._eng.wait_ge(sem, v)

    def add_instruction(self, ins, **kw):
        ev = _Ev(self._chain, "op", "raw")
        self._rec.add(ev)
        return Shim(self._rec, self._eng.add_instruction(ins, **kw), ev)

    def __getattr__(self, opname):
        eng = self._eng
        fn = getattr(eng, opname)
        rec = self._rec
        chain = self._chain

        def call(*args, **kwargs):
            is_dma = opname in EngW._DMA
            ev = _Ev(chain, "dma_issue" if is_dma else "op", opname)
            aps = []
            for i, a in enumerate(args):
                if _is_ap(a):
                    aps.append((i, a))
            first_w = True
            for i, a in aps:
                f = _foot(rec, a)
                if first_w and i == 0:
                    ev.writes.append(f)
                    first_w = False
                else:
                    ev.reads.append(f)
            for k, a in kwargs.items():
                if _is_ap(a):
                    f = _foot(rec, a)
                    if k in ("accum_out", "out", "out_ap"):
                        ev.writes.append(f)
                    else:
                        ev.reads.append(f)
            if is_dma:
                rec.dma_n += 1
                done = _Ev(f"d{chain}{rec.dma_n}", "dma_done", opname)
                done.reads = ev.reads
                done.writes = ev.writes
                done.issue_ref = ev
                rec.add(ev)
                rec.add(done)
                instr = fn(*args, **kwargs)
                return Shim(rec, instr, done)
            rec.add(ev)
            instr = fn(*args, **kwargs)
            return Shim(rec, instr, ev)

        return call



def build_core(record=False, stop_phase=10**9, dump=None):
    nc = bass.Bass("TRN2", target_bir_lowering=False)
    rec = Recorder() if record else None

    x_h = nc.dram_tensor("x", [NB * T, D], F32, kind="ExternalInput").ap()
    out_h = nc.dram_tensor("out", [NB * T, D], F32, kind="ExternalOutput").ap()
    wnames = ["wq", "wk", "wv", "wo", "fq", "fk", "fv", "fo"]
    w_h = {n: nc.dram_tensor(n, [D, D], F32, kind="ExternalInput").ap() for n in wnames}
    col_h = {n: nc.dram_tensor(n, [128, NF], F32, kind="ExternalInput").ap()
             for n in ["fnw", "fnb", "fqb", "fkb"]}
    row_h = {n: nc.dram_tensor(n, [D], F32, kind="ExternalInput").ap()
             for n in ["fvb", "fob"]}

    es = ExitStack()
    sbuf = lambda name, shape, dt=BF16: es.enter_context(nc.sbuf_tensor(name, shape, dt))
    psum = lambda name, shape, dt=F32: es.enter_context(nc.psum_tensor(name, shape, dt))

    x_res = sbuf("x_res", [128, NT * 1024], F32)
    xnT = sbuf("xnT", [128, NF * T])
    yT = sbuf("yT", [128, NF * T])
    qT = sbuf("qT", [128, 2 * T])
    kT = sbuf("kT", [128, 2 * T])
    vaug = sbuf("vaug", [128, NT * 260])
    asq = sbuf("asq", [128, 2 * 1024])
    ypair = sbuf("ypair", [128, NT * 128])
    absq = ypair
    wtq = sbuf("wtq", [128, NF * 256])
    wtk = sbuf("wtk", [128, NF * 256])
    wtv = sbuf("wtv", [128, NF * 256])
    wto = sbuf("wto", [128, NF * 1024])
    xn2scr = sbuf("xn2scr", [128, 1024], F32)
    sqscr = sbuf("sqscr", [128, 1024], F32)
    xnbuf = [ypair[:, 0:1024], ypair[:, 1024:2048]]
    outscr = [sqscr, xn2scr]
    boundrow = sbuf("boundrow", [65, T])
    kmaxcol = sbuf("kmaxcol", [128, 2])
    stat = sbuf("stat", [128, 64], F32)
    statB = sbuf("statB", [128, 8], F32)
    zbuf = sbuf("zbuf", [128, 16], F32)
    rz = sbuf("rz", [128, 16], F32)
    vsum_sb = sbuf("vsum_sb", [65, 16])
    vsumT_sb = sbuf("vsumT_sb", [16, 65])
    cmask = sbuf("cmask", [128, 128])
    identb = sbuf("identb", [128, 128])
    ones16 = sbuf("ones16", [16, 128])
    negrow = sbuf("negrow", [65, 128])
    seedrow = sbuf("seedrow", [1, 455])
    zerorow = sbuf("zerorow", [1, 388])
    vbrow = sbuf("vbrow", [1, D])
    obrow = sbuf("obrow", [1, D])
    fqbc = sbuf("fqbc", [128, NF], F32)
    fkbc = sbuf("fkbc", [128, NF], F32)
    fnwc = sbuf("fnwc", [128, NF], F32)
    fnbc = sbuf("fnbc", [128, NF], F32)
    ceps = sbuf("ceps", [128, 1], F32)
    spcr = sbuf("spcr", [128, 2], F32)

    ps0 = psum("ps0", [128, 1024])
    ps1 = psum("ps1", [128, 1024])
    pv = [psum(f"pv{i}", [128, 512]) for i in range(3)]
    pt = psum("pt", [128, 1024], BF16)

    sX = es.enter_context(nc.semaphore("sX"))
    sXC = es.enter_context(nc.semaphore("sXC"))
    sRow = es.enter_context(nc.semaphore("sRow"))
    sW = es.enter_context(nc.semaphore("sW"))
    sWO = es.enter_context(nc.semaphore("sWO"))
    sOutP = [es.enter_context(nc.semaphore("sOutP0")),
             es.enter_context(nc.semaphore("sOutP1"))]
    sPE = es.enter_context(nc.semaphore("sPE"))
    sACT = es.enter_context(nc.semaphore("sACT"))
    sDVE = es.enter_context(nc.semaphore("sDVE"))
    sPool = es.enter_context(nc.semaphore("sPool"))

    cnt = {id(s): 0 for s in [sX, sXC, sRow, sW, sWO, sOutP[0], sOutP[1],
                              sPE, sACT, sDVE, sPool]}

    def inc(instr, s, n=1):
        instr.then_inc(s, n)
        cnt[id(s)] += n
        return cnt[id(s)]

    def cur(s):
        return cnt[id(s)]

    def wrap(eng, chain):
        return EngW(rec, eng, chain) if rec is not None else eng

    def tag(name):
        if rec is not None:
            rec.cur_phase = name

    def sp(v):
        v.tensor_copy(spcr[:, 0:1], spcr[:, 1:2])

    def w_slice_src(nm, g_i):
        return bass.AP(w_h[nm].tensor, g_i * 256,
                       [[D, 128], [128 * D, NF], [1, 256]])

    def w_full_src(nm):
        return bass.AP(w_h[nm].tensor, 0,
                       [[D, 128], [128 * D, NF], [1, 1024]])

    wtq_dst = bass.AP(wtq.tensor if hasattr(wtq, "tensor") else wtq, 0,
                      [[NF * 256, 128], [256, NF], [1, 256]])
    wtk_dst = bass.AP(wtk.tensor if hasattr(wtk, "tensor") else wtk, 0,
                      [[NF * 256, 128], [256, NF], [1, 256]])
    wtv_dst = bass.AP(wtv.tensor if hasattr(wtv, "tensor") else wtv, 0,
                      [[NF * 256, 128], [256, NF], [1, 256]])
    wto_dst = bass.AP(wto.tensor if hasattr(wto, "tensor") else wto, 0,
                      [[NF * 1024, 128], [1024, NF], [1, 1024]])

    def wt_group(g, stage, g_i):
        names = ("wq", "wk", "wv") if stage == 0 else ("fq", "fk", "fv")
        for nm, dst in zip(names, (wtq_dst, wtk_dst, wtv_dst)):
            inc(g.dma_start(dst, w_slice_src(nm, g_i)), sW, 16)

    def wt_out(g, stage):
        nm = "wo" if stage == 0 else "fo"
        inc(g.dma_start(wto_dst, w_full_src(nm)), sWO, 16)

    with nc.Block() as blk:
        @blk.sync
        def _(y_):
            y = wrap(y_, "Y")
            src = x_h[0:T, :].rearrange("(t p) f -> p t f", p=128)
            dst = bass.AP(x_res.tensor if hasattr(x_res, "tensor") else x_res,
                          0, [[NT * 1024, 128], [1024, NT], [1, 1024]])
            inc(y.dma_start(dst, src), sX, 16)
            for n, dstc in [("fqb", fqbc), ("fkb", fkbc),
                            ("fnw", fnwc), ("fnb", fnbc)]:
                inc(y.dma_start(dstc[:, :], col_h[n]), sXC, 16)

        @blk.gpsimd
        def _(g_):
            g = wrap(g_, "G")
            inc(g.dma_start(vbrow[0:1, :],
                            row_h["fvb"].rearrange("(a f) -> a f", a=1)), sRow, 16)
            inc(g.dma_start(obrow[0:1, :],
                            row_h["fob"].rearrange("(a f) -> a f", a=1)), sRow, 16)
            wt_group(g, 0, 0)
            wt_out(g, 0)
            g.memset(cmask[:, :], 1.0)
            g.affine_select(out=cmask[:, :], in_=cmask[:, :], compare_op=ALU.is_ge,
                            fill=0.0, base=0, pattern=[[1, 128]], channel_multiplier=-1)
            g.memset(identb[:, :], 0.0)
            g.affine_select(out=identb[:, :], in_=identb[:, :], compare_op=ALU.not_equal,
                            fill=1.0, base=0, pattern=[[-1, 128]], channel_multiplier=1)
            g.memset(ones16[:, :], 1.0)
            g.memset(negrow[:, :], -1.0)
            g.memset(ceps[:, :], LNEPS)
            g.memset(zerorow[:, :], 0.0)
            g.memset(seedrow[:, :], 0.0)
            last = None
            for j7 in range(7):
                last = g.memset(seedrow[0:1, j7 * 65 + 64: j7 * 65 + 65], EPS2)
            inc(last, sPool)

        nc.const_aps.register(F32, LNEPS, ceps[:, 0:1])

        @blk.scalar
        def _(s_):
            s = wrap(s_, "S")
            s.add_instruction(mybir.InstLoadActFuncSet(
                name=nc.get_next_instruction_name(),
                act_func_set_id=ACT_SET_LN_EXP, ins=[], outs=[]))

    def ln_stats(x_target=0, out_target=None):
        with nc.Block() as blk:
            dB = cur(sDVE)

            @blk.vector
            def _(v_):
                v = wrap(v_, "V")
                if x_target:
                    v.wait_ge(sX, x_target)
                if out_target is not None:
                    v.wait_ge(sOutP[0], out_target[0])
                    v.wait_ge(sOutP[1], out_target[1])
                for i in range(NT):
                    xt = x_res[:, i * 1024:(i + 1) * 1024]
                    v.scalar_tensor_tensor(sqscr[:, :], xt, 1.0, xt, ALU.mult,
                                           ALU.mult, accum_out=stat[:, 16 + i:17 + i])
                    v.reduce_sum(stat[:, i:i + 1], xt, axis=AX)
                sp(v)
                v.tensor_scalar(stat[:, 0:16], stat[:, 0:16], 1.0 / D, None, ALU.mult)
                sp(v)
                v.tensor_mul(stat[:, 48:64], stat[:, 0:16], stat[:, 0:16])
                sp(v)
                inc(v.scalar_tensor_tensor(stat[:, 16:32], stat[:, 16:32], 1.0 / D,
                                           stat[:, 48:64], ALU.mult, ALU.subtract), sDVE)

            @blk.scalar
            def _(s_):
                s = wrap(s_, "S")
                s.wait_ge(sDVE, cur(sDVE))
                s.activation(stat[:, 48:64], stat[:, 16:32], AF.Ln, bias=LNEPS, scale=1.0)
                inc(s.activation(stat[:, 32:48], stat[:, 48:64], AF.Exp,
                                 bias=0.0, scale=-0.5), sACT)
            _ = dB

    def ln1_apply(pool_memset_stage=None, pool_wait_pe=0):
        with nc.Block() as blk:
            pB, dB, aB = cur(sPE), cur(sDVE), cur(sACT)

            @blk.tensor
            def _(t_):
                t = wrap(t_, "T")
                t.wait_ge(sPool, 1)
                for i in range(NT):
                    t.wait_ge(sDVE, dB + 2 * i + 1)
                    mm = None
                    for j in range(NF):
                        mm = t.transpose(pt[:, j * 128:(j + 1) * 128],
                                         ypair[:, (i % 2) * 1024 + j * 128:
                                               (i % 2) * 1024 + (j + 1) * 128],
                                         identb[:, :])
                    inc(mm, sPE)

            @blk.vector
            def _(v_):
                v = wrap(v_, "V")
                v.wait_ge(sACT, aB)
                for i in range(NT):
                    xt = x_res[:, i * 1024:(i + 1) * 1024]
                    inc(v.tensor_scalar(xnbuf[i % 2], xt, stat[:, i:i + 1],
                                        stat[:, 32 + i:33 + i], ALU.subtract,
                                        ALU.mult), sDVE)
                    v.wait_ge(sPE, pB + i + 1)
                    dst = bass.AP(xnT.tensor if hasattr(xnT, "tensor") else xnT,
                                  i * 128, [[NF * T, 128], [T, NF], [1, 128]])
                    inc(v.tensor_copy(dst, pt[:, :]), sDVE)

            if pool_memset_stage is not None:
                @blk.gpsimd
                def _(g_):
                    g = wrap(g_, "G")
                    g.wait_ge(sPE, pool_wait_pe)
                    last = None
                    if pool_memset_stage == 0:
                        for i in range(NT):
                            for hh in range(4):
                                c0 = i * 260 + hh * 65 + 64
                                last = g.memset(vaug[:, c0:c0 + 1], 1.0)
                    else:
                        for i in range(NT):
                            for j2 in range(2):
                                c0 = i * 260 + j2 * 130 + 128
                                last = g.memset(vaug[:, c0:c0 + 1], 1.0)
                    inc(last, sPool)

    def ln2_apply(pool_wait_pe=0, prefetch=None):
        with nc.Block() as blk:
            pB, dB, aB = cur(sPE), cur(sDVE), cur(sACT)

            @blk.tensor
            def _(t_):
                t = wrap(t_, "T")
                for i in range(NT):
                    t.wait_ge(sDVE, dB + 4 * i + 2)
                    mm = None
                    for j in range(NF):
                        mm = t.transpose(pt[:, j * 128:(j + 1) * 128],
                                         ypair[:, (i % 2) * 1024 + j * 128:
                                               (i % 2) * 1024 + (j + 1) * 128],
                                         identb[:, :])
                    inc(mm, sPE)

            @blk.scalar
            def _(s_):
                s = wrap(s_, "S")
                for i in range(NT):
                    s.wait_ge(sDVE, dB + 4 * i + 1)
                    s.activation(statB[:, 2:3], statB[:, 1:2], AF.Ln,
                                 bias=LNEPS, scale=1.0)
                    inc(s.activation(statB[:, 3:4], statB[:, 2:3], AF.Exp,
                                     bias=0.0, scale=-0.5), sACT)

            @blk.vector
            def _(v_):
                v = wrap(v_, "V")
                v.wait_ge(sXC, 16 * 4)
                v.wait_ge(sACT, aB)
                for i in range(NT):
                    xt = x_res[:, i * 1024:(i + 1) * 1024]
                    v.tensor_scalar(xn2scr[:, :], xt, stat[:, i:i + 1],
                                    stat[:, 32 + i:33 + i], ALU.subtract, ALU.mult)
                    sp(v)
                    v.scalar_tensor_tensor(sqscr[:, :], xn2scr[:, :], 1.0,
                                           xn2scr[:, :], ALU.mult, ALU.mult,
                                           accum_out=statB[:, 4:5])
                    v.reduce_sum(statB[:, 5:6], xn2scr[:, :], axis=AX)
                    sp(v)
                    v.tensor_scalar(statB[:, 0:1], statB[:, 5:6], 1.0 / D, None, ALU.mult)
                    sp(v)
                    v.tensor_mul(statB[:, 6:7], statB[:, 0:1], statB[:, 0:1])
                    sp(v)
                    inc(v.scalar_tensor_tensor(statB[:, 1:2], statB[:, 4:5], 1.0 / D,
                                               statB[:, 6:7], ALU.mult, ALU.subtract),
                        sDVE)
                    v.wait_ge(sACT, aB + i + 1)
                    inc(v.tensor_scalar(xnbuf[i % 2], xn2scr[:, :], statB[:, 0:1],
                                        statB[:, 3:4], ALU.subtract, ALU.mult), sDVE)
                    inc(v.tensor_add(xt, xt, xn2scr[:, :]), sDVE)
                    v.wait_ge(sPE, pB + i + 1)
                    mm = None
                    for j in range(NF):
                        mm = v.tensor_scalar(xnT[:, j * T + i * 128:j * T + (i + 1) * 128],
                                             pt[:, j * 128:(j + 1) * 128],
                                             fnwc[:, j:j + 1], fnbc[:, j:j + 1],
                                             ALU.mult, ALU.add)
                    inc(mm, sDVE)

            @blk.gpsimd
            def _(g_):
                g = wrap(g_, "G")
                g.wait_ge(sPE, pool_wait_pe)
                last = None
                for i in range(NT):
                    for j2 in range(2):
                        c0 = i * 260 + j2 * 130 + 128
                        last = g.memset(vaug[:, c0:c0 + 1], 1.0)
                inc(last, sPool)
                if prefetch is not None:
                    prefetch(g)

    def qkv_phase(stage, g_i, wt_target, pool_target, prefetch=None):
        with nc.Block() as blk:
            pB, dB = cur(sPE), cur(sDVE)
            chains = [(proj, d, c) for proj in (0, 1) for d in range(2)
                      for c in range(4)]
            nqk = len(chains)

            def nK(c):
                return 8 if stage == 0 else [8, 4, 2, 1][c]

            @blk.tensor
            def _(t_):
                t = wrap(t_, "T")
                t.wait_ge(sW, wt_target)
                if stage == 1:
                    t.wait_ge(sRow, 32)
                t.wait_ge(sPool, pool_target)
                t.wait_ge(sDVE, dB)
                for idx, (proj, d, c) in enumerate(chains):
                    if idx >= 3:
                        t.wait_ge(sDVE, dB + idx - 2)
                    w = wtq if proj == 0 else wtk
                    dest = pv[idx % 3]
                    mm = None
                    for kk in range(nK(c)):
                        mm = t.matmul(dest[:, 0:512],
                                 w[:, kk * 256 + d * 128: kk * 256 + (d + 1) * 128],
                                 xnT[:, kk * T + c * 512: kk * T + (c + 1) * 512],
                                 start=(kk == 0), stop=(kk == nK(c) - 1))
                    inc(mm, sPE)
                for i in range(NT):
                    idx = nqk + i
                    t.wait_ge(sDVE, dB + idx - 2)
                    dest = pv[idx % 3]
                    c = i // 4
                    if stage == 1:
                        t.matmul(dest[:, 0:256], ones16[0:1, 0:128],
                                 vbrow[0:1, g_i * 256:(g_i + 1) * 256],
                                 start=True, stop=False)
                    mm = None
                    for kk in range(nK(c)):
                        mm = t.matmul(dest[:, 0:256],
                                 xnT[:, kk * T + i * 128: kk * T + (i + 1) * 128],
                                 wtv[:, kk * 256:(kk + 1) * 256],
                                 start=(stage == 0 and kk == 0),
                                 stop=(kk == nK(c) - 1))
                    inc(mm, sPE)

            @blk.vector
            def _(v_):
                v = wrap(v_, "V")
                if stage == 1:
                    v.wait_ge(sXC, 16 * 4)
                for idx, (proj, d, c) in enumerate(chains):
                    v.wait_ge(sPE, pB + idx + 1)
                    src = pv[idx % 3][:, 0:512]
                    dstT = qT if proj == 0 else kT
                    dcol = d * T + c * 512
                    if stage == 0:
                        if proj == 0:
                            mm = v.tensor_scalar(dstT[:, dcol:dcol + 512], src,
                                                 0.125, None, ALU.mult)
                        else:
                            mm = v.tensor_copy(dstT[:, dcol:dcol + 512], src)
                    else:
                        bc = (fqbc if proj == 0 else fkbc)[:, g_i * 2 + d:g_i * 2 + d + 1]
                        mm = v.tensor_scalar(dstT[:, dcol:dcol + 512], src,
                                             bc, None, ALU.add)
                    inc(mm, sDVE)
                for i in range(NT):
                    idx = nqk + i
                    v.wait_ge(sPE, pB + idx + 1)
                    src = pv[idx % 3][:, 0:256]
                    if stage == 0:
                        dst = bass.AP(vaug.tensor if hasattr(vaug, "tensor") else vaug,
                                      i * 260, [[NT * 260, 128], [65, 4], [1, 64]])
                    else:
                        dst = bass.AP(vaug.tensor if hasattr(vaug, "tensor") else vaug,
                                      i * 260, [[NT * 260, 128], [130, 2], [1, 128]])
                    inc(v.tensor_copy(dst, src), sDVE)

            if prefetch is not None:
                @blk.gpsimd
                def _(g_):
                    g = wrap(g_, "G")
                    prefetch(g)

    def att_A(g_i, prefetch=None, prefetch_wait_pe=None):
        with nc.Block() as blk:
            pB, dB, aB = cur(sPE), cur(sDVE), cur(sACT)
            strips = []
            for k in range(NT):
                for s in range(k * 128 // 1024, 2):
                    tq0 = max(k * 128, s * 1024)
                    strips.append((k, s, tq0, (s + 1) * 1024 - tq0, s == k // 8))
            nstr = len(strips)

            heads = []
            pe_c, act_c, dve_c = pB, aB, dB
            for j in range(4):
                h = {"score": {}, "av": {}, "evac": {}, "mask": {}}
                pe_c += 1
                h["score"][strips[0][0:2]] = pe_c
                pe_c += 1
                h["seed"] = pe_c
                for n, (k, s, tq0, w, dg) in enumerate(strips):
                    if n + 1 < nstr:
                        pe_c += 1
                        h["score"][strips[n + 1][0:2]] = pe_c
                    pe_c += 1
                    h["av"][(k, s)] = pe_c
                for (k, s, tq0, w, dg) in strips:
                    act_c += 1
                    h["evac"][(k, s)] = act_c
                    if dg:
                        dve_c += 1
                        h["mask"][k] = dve_c
                dve_c += 1
                h["vsum_evac"] = dve_c
                pe_c += 1
                h["vsumT_tr"] = pe_c
                dve_c += 1
                h["vsumT_evac"] = dve_c
                pe_c += 1
                h["headdone"] = pe_c
                dve_c += 1
                h["division"] = dve_c
                if j % 2 == 1:
                    h["pairtr"] = []
                    for r in range(2):
                        pe_c += 1
                        dve_c += 1
                        h["pairtr"].append((pe_c, dve_c))
                heads.append(h)

            def bank_of(i):
                return (0, i) if i < 7 else ((1, i - 7) if i < 14 else (2, i - 14))

            @blk.tensor
            def _(t_):
                t = wrap(t_, "T")
                scnt = 0
                for j in range(4):
                    h = heads[j]
                    d, rr = j // 2, (j % 2) * 64
                    if j == 0:
                        t.wait_ge(sDVE, dB)

                    def scoreA(n, scnt_n):
                        k, s, tq0, w, dg = strips[n]
                        psP = (ps0, ps1)[scnt_n % 2]
                        if scnt_n >= 2:
                            t.wait_ge(sACT, aB + scnt_n - 1)
                        r0 = tq0 - s * 1024
                        mm = None
                        for c0, c1 in ((r0, 512), (max(r0, 512), 1024)):
                            if c0 < c1:
                                mm = t.matmul(psP[:, c0:c1],
                                         kT[rr:rr + 64,
                                            d * T + k * 128: d * T + (k + 1) * 128],
                                         qT[rr:rr + 64,
                                            d * T + s * 1024 + c0: d * T + s * 1024 + c1],
                                         start=True, stop=True)
                        inc(mm, sPE)
                        assert cur(sPE) == h["score"][(k, s)]

                    scoreA(0, scnt)
                    if j > 0:
                        t.wait_ge(sDVE, heads[j - 1]["division"])
                    mm = None
                    for pvx, w in ((pv[0], 455), (pv[1], 455), (pv[2], 130)):
                        mm = t.matmul(pvx[:, 0:w], ones16[0:1, 0:128],
                                      seedrow[0:1, 0:w], start=True, stop=False)
                    inc(mm, sPE)
                    assert cur(sPE) == h["seed"]
                    for n, (k, s, tq0, w, dg) in enumerate(strips):
                        if n + 1 < nstr:
                            scoreA(n + 1, scnt + 1)
                        par = scnt % 2
                        if dg:
                            t.wait_ge(sDVE, h["mask"][k])
                        else:
                            t.wait_ge(sACT, h["evac"][(k, s)])
                        vsl = vaug[:, k * 260 + j * 65: k * 260 + j * 65 + 65]
                        i_lo = max(k, 8 * s)
                        mm = None
                        for i in range(i_lo, 8 * s + 8):
                            bk, off = bank_of(i)
                            mm = t.matmul(pv[bk][:, off * 65: off * 65 + 65],
                                     asq[:, par * 1024 + i * 128 - tq0:
                                         par * 1024 + (i + 1) * 128 - tq0],
                                     vsl, start=False, stop=False)
                        if dg:
                            bk, off = bank_of(k)
                            t.matmul(pv[bk][:, off * 65: off * 65 + 65],
                                     cmask[:, :], vsl, start=False, stop=False)
                            mm = t.matmul(pv[2][0:65, 200 + k: 201 + k], vsl,
                                     cmask[:, 127:128], start=False, stop=False)
                        inc(mm, sPE)
                        assert cur(sPE) == h["av"][(k, s)]
                        scnt += 1
                    t.wait_ge(sDVE, h["vsum_evac"])
                    t.wait_ge(sACT, aB + nstr * (j + 1))
                    inc(t.transpose(pt[0:16, 0:65], vsum_sb[:, :],
                                    identb[0:65, 0:65]), sPE)
                    assert cur(sPE) == h["vsumT_tr"]
                    t.wait_ge(sDVE, h["vsumT_evac"])
                    mm = None
                    for i in range(1, NT):
                        bk, off = bank_of(i)
                        mm = t.matmul(pv[bk][:, off * 65: off * 65 + 65],
                                      ones16[0:i, :], vsumT_sb[0:i, 0:65],
                                      start=False, stop=True)
                    inc(mm, sPE)
                    assert cur(sPE) == h["headdone"]
                    if j % 2 == 1:
                        t.wait_ge(sDVE, h["division"])
                        for r in range(2):
                            if r == 1:
                                t.wait_ge(sDVE, h["pairtr"][0][1])
                            mm = None
                            for i8 in range(8):
                                it = r * 8 + i8
                                mm = t.transpose(pt[:, i8 * 128:(i8 + 1) * 128],
                                                 ypair[:, it * 128:(it + 1) * 128],
                                                 identb[:, :])
                            inc(mm, sPE)
                            assert cur(sPE) == h["pairtr"][r][0]

            @blk.scalar
            def _(s_c):
                s_ = wrap(s_c, "S")
                scnt = 0
                for j in range(4):
                    h = heads[j]
                    for (k, s, tq0, w, dg) in strips:
                        s_.wait_ge(sPE, h["score"][(k, s)])
                        if scnt >= 2:
                            gk, gs = (strips * 4)[scnt - 2][0:2]
                            jj = (scnt - 2) // nstr
                            s_.wait_ge(sPE, heads[jj]["av"][(gk, gs)])
                        par = scnt % 2
                        r0 = tq0 - s * 1024
                        psP = (ps0, ps1)[scnt % 2]
                        mm = s_.activation(asq[:, par * 1024: par * 1024 + w],
                                           psP[:, r0:1024], AF.Square,
                                           bias=1.0, scale=1.0)
                        inc(mm, sACT)
                        assert cur(sACT) == h["evac"][(k, s)]
                        scnt += 1

            @blk.vector
            def _(v_):
                v = wrap(v_, "V")
                scnt = 0
                for j in range(4):
                    h = heads[j]
                    for (k, s, tq0, w, dg) in strips:
                        if dg:
                            v.wait_ge(sACT, h["evac"][(k, s)])
                            par = scnt % 2
                            inc(v.tensor_mul(asq[:, par * 1024: par * 1024 + 128],
                                             asq[:, par * 1024: par * 1024 + 128],
                                             cmask[:, :]), sDVE)
                            assert cur(sDVE) == h["mask"][k]
                        scnt += 1
                    v.wait_ge(sPE, h["av"][(15, 1)])
                    inc(v.tensor_copy(vsum_sb[:, :], pv[2][0:65, 200:216]), sDVE)
                    assert cur(sDVE) == h["vsum_evac"]
                    v.wait_ge(sPE, h["vsumT_tr"])
                    inc(v.tensor_copy(vsumT_sb[:, :], pt[0:16, 0:65]), sDVE)
                    assert cur(sDVE) == h["vsumT_evac"]
                    v.wait_ge(sPE, h["headdone"])
                    v.tensor_copy(zbuf[:, 0:7],
                                  bass.AP(pv[0].tensor if hasattr(pv[0], "tensor") else pv[0],
                                          64, [[512, 128], [65, 7], [1, 1]]))
                    v.tensor_copy(zbuf[:, 7:14],
                                  bass.AP(pv[1].tensor if hasattr(pv[1], "tensor") else pv[1],
                                          64, [[512, 128], [65, 7], [1, 1]]))
                    v.tensor_copy(zbuf[:, 14:16],
                                  bass.AP(pv[2].tensor if hasattr(pv[2], "tensor") else pv[2],
                                          64, [[512, 128], [65, 2], [1, 1]]))
                    sp(v)
                    v.reciprocal(rz[:, :], zbuf[:, :])
                    sp(v)
                    p_off = (j % 2) * 64
                    mm = None
                    for i in range(NT):
                        bk, off = bank_of(i)
                        mm = v.tensor_scalar(ypair[:, i * 128 + p_off: i * 128 + p_off + 64],
                                             pv[bk][:, off * 65: off * 65 + 64],
                                             rz[:, i:i + 1], None, ALU.mult)
                    inc(mm, sDVE)
                    assert cur(sDVE) == h["division"]
                    if j % 2 == 1:
                        pr = g_i * 2 + j // 2
                        for r in range(2):
                            v.wait_ge(sPE, h["pairtr"][r][0])
                            mm = v.tensor_copy(
                                yT[:, pr * T + r * 1024: pr * T + (r + 1) * 1024],
                                pt[:, :])
                            inc(mm, sDVE)
                            assert cur(sDVE) == h["pairtr"][r][1]

            if prefetch is not None:
                @blk.gpsimd
                def _(g_):
                    g = wrap(g_, "G")
                    if prefetch_wait_pe:
                        g.wait_ge(sPE, prefetch_wait_pe)
                    prefetch(g)

    def wo_phase():
        with nc.Block() as blk:
            pB, dB = cur(sPE), cur(sDVE)
            wt_target = cur(sWO)

            @blk.tensor
            def _(t_):
                t = wrap(t_, "T")
                t.wait_ge(sWO, wt_target)
                t.wait_ge(sDVE, dB)
                idx = 0
                for i in range(NT):
                    for ch in range(2):
                        if idx >= 3:
                            t.wait_ge(sDVE, dB + idx - 2)
                        dest = pv[idx % 3]
                        mm = None
                        for j in range(NF):
                            mm = t.matmul(dest[:, 0:512],
                                     yT[:, j * T + i * 128: j * T + (i + 1) * 128],
                                     wto[:, j * 1024 + ch * 512: j * 1024 + (ch + 1) * 512],
                                     start=(j == 0), stop=(j == NF - 1))
                        inc(mm, sPE)
                        idx += 1

            @blk.vector
            def _(v_):
                v = wrap(v_, "V")
                idx = 0
                for i in range(NT):
                    for ch in range(2):
                        v.wait_ge(sPE, pB + idx + 1)
                        xsl = x_res[:, i * 1024 + ch * 512: i * 1024 + (ch + 1) * 512]
                        inc(v.tensor_add(xsl, xsl, pv[idx % 3][:, 0:512]), sDVE)
                        idx += 1

    def bounds_phase(g_i):
        with nc.Block() as blk:
            pB, dB, aB = cur(sPE), cur(sDVE), cur(sACT)

            @blk.tensor
            def _(t_):
                t = wrap(t_, "T")
                t.wait_ge(sACT, aB)
                t.wait_ge(sDVE, dB + 1)
                mm = None
                for c2 in range(2):
                    mm = t.matmul(ps0[0:1, c2 * 512:(c2 + 1) * 512], kmaxcol[:, 0:1],
                             absq[:, c2 * 512:(c2 + 1) * 512], start=True, stop=True)
                inc(mm, sPE)
                for c2 in range(2):
                    mm = t.matmul(ps1[0:1, c2 * 512:(c2 + 1) * 512], kmaxcol[:, 0:1],
                             absq[:, 1024 + c2 * 512: 1024 + (c2 + 1) * 512],
                             start=True, stop=True)
                inc(mm, sPE)
                t.wait_ge(sDVE, dB + 4)
                for c2 in range(2):
                    mm = t.matmul(ps0[0:1, c2 * 512:(c2 + 1) * 512], kmaxcol[:, 1:2],
                             absq[:, c2 * 512:(c2 + 1) * 512], start=True, stop=True)
                inc(mm, sPE)
                for c2 in range(2):
                    mm = t.matmul(ps1[0:1, c2 * 512:(c2 + 1) * 512], kmaxcol[:, 1:2],
                             absq[:, 1024 + c2 * 512: 1024 + (c2 + 1) * 512],
                             start=True, stop=True)
                inc(mm, sPE)

            @blk.vector
            def _(v_):
                v = wrap(v_, "V")
                v.wait_ge(sPE, pB)
                for j in range(2):
                    v.reduce_max(kmaxcol[:, j:j + 1], kT[:, j * T:(j + 1) * T],
                                 axis=AX, apply_absolute_value=True)
                inc(v.scalar_tensor_tensor(absq[:, 0:2048], qT[:, 0:T], -1.0,
                                           qT[:, 0:T], ALU.mult, ALU.max), sDVE)
                v.wait_ge(sPE, pB + 2)
                inc(v.tensor_copy(boundrow[0:1, 0:1024], ps0[0:1, 0:1024]), sDVE)
                inc(v.tensor_copy(boundrow[0:1, 1024:2048], ps1[0:1, 0:1024]), sDVE)
                inc(v.scalar_tensor_tensor(absq[:, 0:2048], qT[:, T:2 * T], -1.0,
                                           qT[:, T:2 * T], ALU.mult, ALU.max), sDVE)
                v.wait_ge(sPE, pB + 4)
                inc(v.tensor_copy(boundrow[64:65, 0:1024], ps0[0:1, 0:1024]), sDVE)
                inc(v.tensor_copy(boundrow[64:65, 1024:2048], ps1[0:1, 0:1024]), sDVE)
            return cur(sDVE)

    def att_B(g_i, bounds_done, prefetch=None, prefetch_wait_pe=None):
        with nc.Block() as blk:
            pB, dB, aB = cur(sPE), cur(sDVE), cur(sACT)
            units = [(j, c2) for j in range(2) for c2 in range(2)]
            plans = []
            pe_c, act_c, dve_c = pB, aB, dB
            trs = {}
            for (j, c2) in units:
                h = {}
                h["score"] = {}
                h["av"] = {}
                h["evac"] = {}
                pe_c += 1
                h["score"][0] = pe_c
                pe_c += 1
                h["seed"] = pe_c
                for k in range(NT):
                    if k + 1 < NT:
                        pe_c += 1
                        h["score"][k + 1] = pe_c
                    pe_c += 1
                    h["av"][k] = pe_c
                for k in range(NT):
                    act_c += 1
                    h["evac"][k] = act_c
                dve_c += 1
                h["division"] = dve_c
                plans.append(h)
                if c2 == 1:
                    lst = []
                    for r in range(2):
                        pe_c += 1
                        dve_c += 1
                        lst.append((pe_c, dve_c))
                    trs[j] = lst

            @blk.tensor
            def _(t_):
                t = wrap(t_, "T")
                for u, (j, c2) in enumerate(units):
                    h = plans[u]
                    if u == 0:
                        t.wait_ge(sDVE, bounds_done)
                    jrow = j * 64

                    def scoreB(k):
                        psP = (ps0, ps1)[k % 2]
                        gidx = u * NT + k
                        if gidx >= 2:
                            t.wait_ge(sACT, aB + gidx - 1)
                        mm = None
                        for half in range(2):
                            t.matmul(psP[:, half * 512:(half + 1) * 512],
                                     negrow[jrow:jrow + 1, 0:128],
                                     boundrow[jrow:jrow + 1,
                                              c2 * 1024 + half * 512:
                                              c2 * 1024 + (half + 1) * 512],
                                     start=True, stop=False)
                            mm = t.matmul(psP[:, half * 512:(half + 1) * 512],
                                     kT[:, j * T + k * 128: j * T + (k + 1) * 128],
                                     qT[:, j * T + c2 * 1024 + half * 512:
                                        j * T + c2 * 1024 + (half + 1) * 512],
                                     start=False, stop=True)
                        inc(mm, sPE)
                        assert cur(sPE) == h["score"][k]

                    scoreB(0)
                    if u > 0:
                        t.wait_ge(sDVE, plans[u - 1]["division"])
                    mm = None
                    for pvx, w in ((pv[0], 387), (pv[1], 387), (pv[2], 258)):
                        mm = t.matmul(pvx[:, 0:w], ones16[0:1, 0:128],
                                      zerorow[0:1, 0:w], start=True, stop=False)
                    inc(mm, sPE)
                    assert cur(sPE) == h["seed"]
                    for k in range(NT):
                        if k + 1 < NT:
                            scoreB(k + 1)
                        t.wait_ge(sACT, h["evac"][k])
                        par = k % 2
                        vsl = vaug[:, k * 260 + j * 130: k * 260 + j * 130 + 129]
                        mm = None
                        for i in range(8):
                            bk, off = (0, i) if i < 3 else ((1, i - 3) if i < 6 else (2, i - 6))
                            mm = t.matmul(pv[bk][:, off * 129: off * 129 + 129],
                                          asq[:, par * 1024 + i * 128: par * 1024 + (i + 1) * 128],
                                          vsl, start=False, stop=(k == NT - 1))
                        inc(mm, sPE)
                        assert cur(sPE) == h["av"][k]
                    if c2 == 1:
                        t.wait_ge(sDVE, h["division"])
                        for r in range(2):
                            if r == 1:
                                t.wait_ge(sDVE, trs[j][0][1])
                            mm = None
                            for i8 in range(8):
                                it = r * 8 + i8
                                mm = t.transpose(pt[:, i8 * 128:(i8 + 1) * 128],
                                                 ypair[:, it * 128:(it + 1) * 128],
                                                 identb[:, :])
                            inc(mm, sPE)
                            assert cur(sPE) == trs[j][r][0]

            @blk.scalar
            def _(s_):
                s = wrap(s_, "S")
                for u, (j, c2) in enumerate(units):
                    h = plans[u]
                    for k in range(NT):
                        par = k % 2
                        s.wait_ge(sPE, h["score"][k])
                        if k >= 2:
                            s.wait_ge(sPE, h["av"][k - 2])
                        elif u > 0:
                            s.wait_ge(sPE, plans[u - 1]["av"][NT - 2 + k])
                        mm = s.activation(asq[:, par * 1024: par * 1024 + 1024],
                                          (ps0, ps1)[k % 2][:, 0:1024], AF.Exp,
                                          bias=0.0, scale=1.0 / 32.0)
                        inc(mm, sACT)
                        assert cur(sACT) == h["evac"][k]

            @blk.vector
            def _(v_):
                v = wrap(v_, "V")
                for u, (j, c2) in enumerate(units):
                    h = plans[u]
                    v.wait_ge(sPE, h["av"][NT - 1])
                    v.tensor_copy(zbuf[:, 0:3],
                                  bass.AP(pv[0].tensor if hasattr(pv[0], "tensor") else pv[0],
                                          128, [[512, 128], [129, 3], [1, 1]]))
                    v.tensor_copy(zbuf[:, 3:6],
                                  bass.AP(pv[1].tensor if hasattr(pv[1], "tensor") else pv[1],
                                          128, [[512, 128], [129, 3], [1, 1]]))
                    v.tensor_copy(zbuf[:, 6:8],
                                  bass.AP(pv[2].tensor if hasattr(pv[2], "tensor") else pv[2],
                                          128, [[512, 128], [129, 2], [1, 1]]))
                    sp(v)
                    v.reciprocal(rz[:, 0:8], zbuf[:, 0:8])
                    sp(v)
                    mm = None
                    for i in range(8):
                        it = c2 * 8 + i
                        bk, off = (0, i) if i < 3 else ((1, i - 3) if i < 6 else (2, i - 6))
                        mm = v.tensor_scalar(
                            ypair[:, it * 128:(it + 1) * 128],
                            pv[bk][:, off * 129: off * 129 + 128],
                            rz[:, i:i + 1], None, ALU.mult)
                    inc(mm, sDVE)
                    assert cur(sDVE) == h["division"]
                    if c2 == 1:
                        pr = g_i * 2 + j
                        for r in range(2):
                            v.wait_ge(sPE, trs[j][r][0])
                            mm = v.tensor_copy(
                                yT[:, pr * T + r * 1024: pr * T + (r + 1) * 1024],
                                pt[:, :])
                            inc(mm, sDVE)
                            assert cur(sDVE) == trs[j][r][1]

            if prefetch is not None:
                @blk.gpsimd
                def _(g_):
                    g = wrap(g_, "G")
                    if prefetch_wait_pe:
                        g.wait_ge(sPE, prefetch_wait_pe)
                    prefetch(g)

    def out_phase(b, xload_slot=None):
        echunks = {0: [(0, 512), (512, 1024)], 1: [(0, 512)], 2: [(0, 256)],
                   3: [(0, 128)]}
        with nc.Block() as blk:
            pB, dB = cur(sPE), cur(sDVE)
            oB = [cur(sOutP[0]), cur(sOutP[1])]
            wt_target = cur(sWO)
            tasks = [(e, i) for e in range(4) for i in range(4 * e, 4 * e + 4)]
            nchunks = sum(len(echunks[e]) for e, i in tasks)

            task_of_chunk = []
            for ti, (e, i) in enumerate(tasks):
                for _ in echunks[e]:
                    task_of_chunk.append(ti)

            @blk.tensor
            def _(t_):
                t = wrap(t_, "T")
                t.wait_ge(sWO, wt_target)
                t.wait_ge(sRow, 32)
                t.wait_ge(sDVE, dB)
                idx = 0
                for e, i in tasks:
                    nK = [8, 4, 2, 1][e]
                    for (o0, o1) in echunks[e]:
                        if idx >= 3:
                            t.wait_ge(sDVE, dB + task_of_chunk[idx - 3] + 1)
                        dest = pv[idx % 3]
                        t.matmul(dest[:, 0:o1 - o0], ones16[0:1, 0:128],
                                 obrow[0:1, o0:o1], start=True, stop=False)
                        mm = None
                        for kk in range(nK):
                            mm = t.matmul(dest[:, 0:o1 - o0],
                                     yT[:, kk * T + i * 128: kk * T + (i + 1) * 128],
                                     wto[:, kk * 1024 + o0: kk * 1024 + o1],
                                     start=False, stop=(kk == nK - 1))
                        inc(mm, sPE)
                        idx += 1

            @blk.vector
            def _(v_):
                v = wrap(v_, "V")
                idx = 0
                for ti, (e, i) in enumerate(tasks):
                    par = ti % 2
                    m_e = 1024 >> e
                    if ti >= 2:
                        v.wait_ge(sOutP[par], oB[par] + (ti // 2) * 16)
                    xt = x_res[:, i * 1024:(i + 1) * 1024]
                    last = None
                    for (o0, o1) in echunks[e]:
                        v.wait_ge(sPE, pB + idx + 1)
                        last = v.tensor_add(outscr[par][:, o0:o1], xt[:, o0:o1],
                                            pv[idx % 3][:, 0:o1 - o0])
                        idx += 1
                    if m_e < 1024:
                        last = v.tensor_copy(outscr[par][:, m_e:1024], xt[:, m_e:1024])
                    inc(last, sDVE)

            @blk.sync
            def _(y_):
                y = wrap(y_, "Y")
                for ti, (e, i) in enumerate(tasks):
                    par = ti % 2
                    y.wait_ge(sDVE, dB + ti + 1)
                    inc(y.dma_start(out_h[b * T + i * 128: b * T + (i + 1) * 128, :],
                                    outscr[par][:, :]), sOutP[par], 16)
                if b == 0:
                    y.wait_ge(sDVE, dB + NT)
                    srcx = x_h[T:2 * T, :].rearrange("(t p) f -> p t f", p=128)
                    dstx = bass.AP(x_res.tensor if hasattr(x_res, "tensor") else x_res,
                                   0, [[NT * 1024, 128], [1024, NT], [1, 1024]])
                    xload_slot[0] = inc(y.dma_start(dstx, srcx), sX, 16)
                else:
                    y.wait_ge(sOutP[0], cur(sOutP[0]))
                    y.wait_ge(sOutP[1], cur(sOutP[1]))

            @blk.gpsimd
            def _(g_):
                g = wrap(g_, "G")
                if b == 0:
                    g.wait_ge(sPE, pB + nchunks)
                    wt_out(g, 0)

    phase_n = [0]

    def ph():
        phase_n[0] += 1
        return phase_n[0] <= stop_phase

    xload_slot = [16]
    out_after_b0 = None
    for b in range(NB):
        xt_target = xload_slot[0]
        if ph():
            tag(f"b{b}ln_stats1")
            ln_stats(x_target=xt_target, out_target=(None if b == 0 else out_after_b0))
        if ph():
            tag(f"b{b}ln1")
            ln1_apply(pool_memset_stage=0, pool_wait_pe=cur(sPE))
        poolA = cur(sPool)
        for g_i in range(4):
            if ph():
                tag(f"b{b}qkvA{g_i}")
                qkv_phase(0, g_i, wt_target=cur(sW), pool_target=poolA)
            if ph():
                tag(f"b{b}attA{g_i}")
                if g_i < 3:
                    att_A(g_i, prefetch=(lambda g, gn=g_i: wt_group(g, 0, gn + 1)),
                          prefetch_wait_pe=cur(sPE))
                else:
                    att_A(g_i, prefetch=(lambda g: wt_group(g, 1, 0)),
                          prefetch_wait_pe=cur(sPE))
        if ph():
            tag(f"b{b}wo")
            wo_phase()
        if ph():
            tag(f"b{b}ln_stats2")
            ln_stats()
        if ph():
            tag(f"b{b}ln2")
            ln2_apply(pool_wait_pe=cur(sPE),
                      prefetch=(lambda g: wt_out(g, 1)))
        poolB = cur(sPool)
        for g_i in range(4):
            if ph():
                tag(f"b{b}qkvB{g_i}")
                qkv_phase(1, g_i, wt_target=cur(sW), pool_target=poolB)
            if ph():
                tag(f"b{b}bounds{g_i}")
                bdone = bounds_phase(g_i)
            if ph():
                tag(f"b{b}attB{g_i}")
                if g_i < 3:
                    att_B(g_i, bdone, prefetch=(lambda g, gn=g_i: wt_group(g, 1, gn + 1)),
                          prefetch_wait_pe=cur(sPE))
                elif b == 0:
                    att_B(g_i, bdone, prefetch=(lambda g: wt_group(g, 0, 0)),
                          prefetch_wait_pe=cur(sPE))
                else:
                    att_B(g_i, bdone)
        if ph():
            tag(f"b{b}out")
            out_phase(b, xload_slot)
        if b == 0:
            out_after_b0 = (cur(sOutP[0]), cur(sOutP[1]))

    with nc.Block() as blk:
        if phase_n[0] > stop_phase:
            @blk.tensor
            def _(t_):
                t = wrap(t_, "T")
                t.wait_ge(sDVE, cur(sDVE))
                t.wait_ge(sACT, cur(sACT))
            @blk.vector
            def _(v_):
                v = wrap(v_, "V")
                v.wait_ge(sPE, cur(sPE))
                v.wait_ge(sPool, cur(sPool))
            @blk.scalar
            def _(s_):
                s = wrap(s_, "S")
                s.wait_ge(sPE, cur(sPE))
            @blk.gpsimd
            def _(g_):
                g = wrap(g_, "G")
                g.wait_ge(sPE, cur(sPE))
            dump_map = {"x_res": (x_res, NT * 1024), "xnT": (xnT, NF * T),
                        "yT": (yT, NF * T), "qT": (qT, 2 * T), "kT": (kT, 2 * T),
                        "vaug": (vaug, NT * 260), "asq": (asq, 2048),
                        "ypair": (ypair, NT * 128), "boundrow": (boundrow, T),
                        "stat": (stat, 64), "wtq": (wtq, NF * 256),
                        "wtv": (wtv, NF * 256), "wto": (wto, NF * 1024)}
            dt_, dw_ = dump_map[dump] if dump else (x_res, NT * 1024)

            @blk.gpsimd
            def _(g_):
                g = wrap(g_, "G")
                g.wait_ge(sX, cur(sX))
                g.wait_ge(sXC, cur(sXC))
                g.wait_ge(sRow, cur(sRow))
                g.wait_ge(sW, cur(sW))
                g.wait_ge(sWO, cur(sWO))
                g.wait_ge(sPE, cur(sPE))
                g.wait_ge(sDVE, cur(sDVE))
                g.wait_ge(sACT, cur(sACT))
                nt_ = (dw_ + 1023) // 1024
                for i in range(NB * NT):
                    ii = i % nt_
                    w = min(1024, dw_ - ii * 1024)
                    inc(g.dma_start(out_h[i * 128:(i + 1) * 128, 0:w],
                                    dt_[:, ii * 1024: ii * 1024 + w]),
                        sOutP[i % 2], 16)
                g.wait_ge(sOutP[0], cur(sOutP[0]))
                g.wait_ge(sOutP[1], cur(sOutP[1]))
        else:
            @blk.sync
            def _(y_):
                y = wrap(y_, "Y")
                y.wait_ge(sOutP[0], cur(sOutP[0]))
                y.wait_ge(sOutP[1], cur(sOutP[1]))

    es.close()
    return (nc, rec) if record else nc



def verify(rec):
    chains = sorted(rec.chains.keys())
    cidx = {c: i for i, c in enumerate(chains)}
    nch = len(chains)
    heads = {c: 0 for c in chains}
    semval = {}
    sem_prefix = {}
    fire_order = []

    races = []

    def join(a, b):
        return [max(x, y) for x, y in zip(a, b)]

    prev_clock = {c: [0] * nch for c in chains}

    progressed = True
    total = sum(len(v) for v in rec.chains.values())
    fired = 0
    while fired < total:
        progressed = False
        for c in chains:
            ch = rec.chains[c]
            while heads[c] < len(ch):
                ev = ch[heads[c]]
                if ev.kind == "dma_done" and ev.issue_ref.fire < 0:
                    break
                ok = True
                for sid, v, nm in ev.waits:
                    if semval.get(sid, 0) < v:
                        ok = False
                        break
                if not ok:
                    break
                clk = list(prev_clock[c])
                for sid, v, nm in ev.waits:
                    lst = sem_prefix.get(sid, [])
                    lo, hi = 0, len(lst)
                    while lo < hi:
                        mid = (lo + hi) // 2
                        if lst[mid][0] >= v:
                            hi = mid
                        else:
                            lo = mid + 1
                    if lo < len(lst):
                        clk = join(clk, lst[lo][1])
                    elif lst:
                        clk = join(clk, lst[-1][1])
                clk[cidx[c]] = heads[c] + 1
                ev.clock = clk
                ev.fire = len(fire_order)
                fire_order.append(ev)
                prev_clock[c] = clk
                for sid, n, nm in ev.incs:
                    semval[sid] = semval.get(sid, 0) + n
                    lst = sem_prefix.setdefault(sid, [])
                    pc = join(lst[-1][1], clk) if lst else clk
                    lst.append((semval[sid], pc))
                heads[c] += 1
                fired += 1
                progressed = True
        if not progressed:
            msgs = []
            for c in chains:
                if heads[c] < len(rec.chains[c]):
                    ev = rec.chains[c][heads[c]]
                    msgs.append(f"  chain {c} stuck at #{heads[c]} {ev.label} "
                                f"waits={[(nm, v, semval.get(sid, 0)) for sid, v, nm in ev.waits]}")
            raise RuntimeError("DEADLOCK:\n" + "\n".join(msgs))

    def hb(a, b):
        return b.clock[cidx[a.chain]] >= a.idx + 1

    sem_issues = []
    for c in chains:
        for ev in rec.chains[c]:
            for sid, v, nm in ev.waits:
                incs = rec.sem_incs.get(sid, [])
                if len({ie.chain for ie, n, iss in incs}) <= 1:
                    continue
                run = 0
                k = len(incs)
                for idx2, (ie, n, iss) in enumerate(incs):
                    run += n
                    if run >= v:
                        k = idx2 + 1
                        break
                for ie, n, iss in incs[k:]:
                    if not hb(ev, iss):
                        sem_issues.append((nm, v, ev.chain, ev.idx, ev.label,
                                           iss.chain, iss.idx, iss.label))
    for s in sem_issues[:40]:
        print("SEM-ORDER", s)
    if sem_issues:
        races.append(("SEM-ORDER-COUNT", len(sem_issues), "", 0, "", "", 0, ""))

    state = {}

    def cells_of(foot):
        name, pr, ivals, is_psum, esz = foot
        pgs = set(p // 64 for p in range(pr[0], pr[1], 64)) | {(pr[1] - 1) // 64}
        out = []
        for lo, hi in ivals:
            for cell in range(lo // 128, (hi - 1) // 128 + 1):
                for pg in pgs:
                    out.append(((name, pg, cell), lo, hi))
        return out

    def overlap(iv1, iv2):
        return iv1[0] < iv2[1] and iv2[0] < iv1[1]

    bank_state = {}

    def access(ev, foot, is_write):
        name, pr, ivals, is_psum, esz = foot
        if is_psum:
            banks = set()
            for lo, hi in ivals:
                b0 = (lo * esz) // 2048
                b1 = ((hi - 1) * esz) // 2048
                banks.update(range(b0, b1 + 1))
            for bk in banks:
                st = bank_state.setdefault((name, bk), [None, {}])
                if ev.chain == "T" and is_write:
                    for c, other in st[1].items():
                        if not hb(other, ev):
                            races.append(("PSUM-BANK", (name, bk), other.chain,
                                          other.idx, other.label, ev.chain,
                                          ev.idx, ev.label))
                    st[0] = ev
                elif ev.chain in ("V", "S"):
                    if st[0] is not None and not hb(st[0], ev):
                        races.append(("PSUM-BANK", (name, bk), st[0].chain,
                                      st[0].idx, st[0].label, ev.chain,
                                      ev.idx, ev.label))
                    st[1][ev.chain] = ev
        for key, lo, hi in cells_of(foot):
            st = state.setdefault(key, {})
            for oc, rec2 in st.items():
                if oc == ev.chain:
                    continue
                ow_iv, ow_ev, or_iv, or_ev = rec2
                if is_write:
                    if ow_ev is not None and any(overlap((lo, hi), i) for i in ow_iv):
                        _chk(ow_ev, ev, races, key, "WW")
                    if or_ev is not None and any(overlap((lo, hi), i) for i in or_iv):
                        _chk(or_ev, ev, races, key, "WR->W")
                else:
                    if ow_ev is not None and any(overlap((lo, hi), i) for i in ow_iv):
                        _chk(ow_ev, ev, races, key, "W->R")
            mine = st.setdefault(ev.chain, [[], None, [], None])
            if is_write:
                mine[0].append((lo, hi))
                mine[1] = ev
            else:
                mine[2].append((lo, hi))
                mine[3] = ev

    def _chk(a, b, races, key, kind):
        aa = a
        bb = b
        if a.kind == "dma_done":
            aa = a
        if b.kind == "dma_done" and b.issue_ref is not None:
            bb = b.issue_ref
        if not hb(aa, bb if bb.clock is not None else b):
            races.append((kind, key, a.chain, a.idx, a.label, b.chain, b.idx, b.label))

    for ev in fire_order:
        if ev.kind == "dma_issue":
            continue
        for f in ev.reads:
            access(ev, f, False)
        for f in ev.writes:
            access(ev, f, True)

    def foot_overlap(fa, fb):
        if fa[0] != fb[0]:
            return False
        if fa[1][1] <= fb[1][0] or fb[1][1] <= fa[1][0]:
            return False
        for a in fa[2]:
            for b in fb[2]:
                if a[0] < b[1] and b[0] < a[1]:
                    return True
        return False

    for chain in ("V",):
        cV = cidx[chain]
        evs = [e for e in rec.chains.get(chain, []) if e.kind == "op"]
        other = [c for c in chains if c != chain and not c.startswith("d")]
        for i in range(len(evs) - 1):
            a, b = evs[i], evs[i + 1]
            hit = False
            for fw in a.writes:
                for fr in b.reads + b.writes:
                    if foot_overlap(fw, fr):
                        hit = True
            if not hit:
                continue
            proven = False
            for c in other:
                lst = rec.chains[c]
                lo, hi = 0, len(lst)
                while lo < hi:
                    mid = (lo + hi) // 2
                    if lst[mid].clock is not None and lst[mid].clock[cV] >= a.idx + 1:
                        hi = mid
                    else:
                        lo = mid + 1
                if lo < len(lst) and b.clock[cidx[c]] >= lo + 1:
                    proven = True
                    break
            if not proven:
                for fw in a.writes:
                    for fr in b.reads + b.writes:
                        if foot_overlap(fw, fr):
                            races.append(("DVE-ADJ", fw[0], a.chain, a.idx,
                                          a.label, b.chain, b.idx, b.label))
    return races



_CACHE = {}


def kernel(**inputs):
    e_cnt = np.asarray(inputs["L0_ffn_router_prob"]).shape[2]
    assert e_cnt == 4, f"kernel built for 4 experts, got {e_cnt}"
    x = np.ascontiguousarray(np.asarray(inputs["L0_attn_hidden_states"],
                                        dtype=np.float32))
    wt = {
        "wq": "L0_attn_Wq", "wk": "L0_attn_Wk", "wv": "L0_attn_Wv",
        "wo": "L0_attn_Wo", "fq": "L0_ffn_q_w", "fk": "L0_ffn_k_w",
        "fv": "L0_ffn_v_w", "fo": "L0_ffn_o_w",
    }
    common = {}
    for k, v in wt.items():
        w = np.asarray(inputs[v], dtype=np.float32)
        common[k] = np.ascontiguousarray(w.T)
    for k, v in [("fnw", "L0_ffn_norm_w"), ("fnb", "L0_ffn_norm_b"),
                 ("fqb", "L0_ffn_q_b"), ("fkb", "L0_ffn_k_b")]:
        a = np.asarray(inputs[v], dtype=np.float32)
        common[k] = np.ascontiguousarray(a.reshape(NF, 128).T)
    for k, v in [("fvb", "L0_ffn_v_b"), ("fob", "L0_ffn_o_b")]:
        common[k] = np.ascontiguousarray(np.asarray(inputs[v], dtype=np.float32))
    stop_phase = _CACHE.get("stop_phase", 10**9)
    dump = _CACHE.get("dump", None)
    key = ("nc", stop_phase, dump)
    if key not in _CACHE:
        _CACHE[key] = build_core(stop_phase=stop_phase, dump=dump)
    nc = _CACHE[key]
    in_maps = []
    for c in range(8):
        m = dict(common)
        m["x"] = np.ascontiguousarray(x[2 * c:2 * c + 2].reshape(NB * T, D))
        in_maps.append(m)
    res = run_bass_kernel_spmd(nc, in_maps, core_ids=list(range(8)))
    out = np.concatenate([r["out"].reshape(NB, T, D) for r in res.results], axis=0)
    return out.astype(np.float32)


if __name__ == "__main__":
    if "--check" in sys.argv:
        nc, rec = build_core(record=True)
        print("built; chains:", {c: len(v) for c, v in rec.chains.items()})
        races = verify(rec)
        print(f"verify: {len(races)} potential races")
        for r in races[:60]:
            print("  RACE", r)
    else:
        nc = build_core()
        print("built OK; instructions:", nc.n_instructions())


# version 08 (final)
# speedup vs baseline: 1.4199x; 1.1639x over previous
import sys
import numpy as np
from contextlib import ExitStack

import concourse.bass as bass
import concourse.mybir as mybir
from concourse.bass_utils import run_bass_kernel_spmd

F32 = mybir.dt.float32
BF16 = mybir.dt.bfloat16
AF = mybir.ActivationFunctionType
ALU = mybir.AluOpType
AX = mybir.AxisListType.X

D = 1024
T = 2048
NT = 16
NF = 8
NB = 2
EPS2 = 2e-6
LNEPS = 1e-5
ACT_SET_LN_EXP = 6



class _Ev:
    __slots__ = ("chain", "idx", "kind", "reads", "writes", "waits", "incs",
                 "label", "issue_ref", "clock", "fire")

    def __init__(self, chain, kind, label=""):
        self.chain = chain
        self.idx = -1
        self.kind = kind
        self.reads = []
        self.writes = []
        self.waits = []
        self.incs = []
        self.label = label
        self.issue_ref = None
        self.clock = None
        self.fire = -1


class Recorder:
    def __init__(self):
        self.chains = {}
        self.sem_owner = {}
        self.tensors = {}
        self.cur_phase = "init"
        self.dma_n = 0
        self.sem_incs = {}

    def chain(self, name):
        return self.chains.setdefault(name, [])

    def add(self, ev):
        ev.label = self.cur_phase + ":" + ev.label
        ch = self.chain(ev.chain)
        ev.idx = len(ch)
        ch.append(ev)
        return ev


def _is_ap(a):
    return hasattr(a, "tensor") and hasattr(a, "offset") and hasattr(a, "ap")


def _foot(rec, ap):
    h = ap.tensor
    name = h.name
    try:
        shape = list(h.shape)
    except Exception:
        shape = None
    dims = [list(d) for d in ap.ap]
    off = ap.offset
    kinds = type(h).__name__ + str(getattr(h, "space", ""))
    if "SB" in kinds or "PSUM" in kinds or "SBUF" in kinds or (
            shape is not None and len(shape) == 2 and shape[0] <= 128 and "DRAM" not in kinds and "DRam" not in kinds):
        F = shape[-1]
        esz = mybir.dt_size(h.dtype) if hasattr(mybir, "dt_size") else (
            2 if h.dtype == BF16 else 4)
        if dims and dims[0][0] == F:
            p0 = off // F
            pcnt = dims[0][1]
            rest = dims[1:]
            base = off % F
        else:
            p0 = off // F
            pcnt = 1
            rest = dims
            base = off % F
        ivals = [(base, base)]
        total = 1
        for stp, cnt in rest[:-1] if rest else []:
            total *= cnt
        if rest:
            inner_stp, inner_cnt = rest[-1]
            outers = rest[:-1]
            combos = [0]
            ok = True
            for stp, cnt in outers:
                if len(combos) * cnt > 4096:
                    ok = False
                    break
                combos = [c + stp * i for c in combos for i in range(cnt)]
            if ok:
                w = abs(inner_stp) * (inner_cnt - 1) + 1
                ivals = sorted((base + c, base + c + w) for c in combos)
            else:
                lo = base
                hi = base
                for stp, cnt in rest:
                    hi += abs(stp) * (cnt - 1)
                ivals = [(lo, hi + 1)]
        else:
            ivals = [(base, base + 1)]
        merged = []
        for lo, hi in ivals:
            if merged and lo <= merged[-1][1]:
                merged[-1] = (merged[-1][0], max(merged[-1][1], hi))
            else:
                merged.append((lo, hi))
        is_psum = "PSUM" in kinds or "Psum" in kinds
        rec.tensors[name] = ("psum" if is_psum else "sbuf", F, esz)
        return (name, (p0, p0 + pcnt), merged, is_psum, esz)
    else:
        lo = off
        hi = off
        for stp, cnt in dims:
            hi += abs(stp) * (cnt - 1)
        rec.tensors[name] = ("dram", None, None)
        return (name, (0, 1), [(lo, hi + 1)], False, 4)


class Shim:
    def __init__(self, rec, instr, ev):
        self._rec = rec
        self._instr = instr
        self._ev = ev

    def then_inc(self, sem, n):
        if self._ev is not None:
            self._ev.incs.append((id(sem), n, getattr(sem, "name", "sem")))
            iss = self._ev.issue_ref if self._ev.kind == "dma_done" else self._ev
            self._rec.sem_incs.setdefault(id(sem), []).append((self._ev, n, iss))
        self._instr.then_inc(sem, n)
        return self


class EngW:
    _DMA = ("dma_start",)

    def __init__(self, rec, eng, chain):
        self._rec = rec
        self._eng = eng
        self._chain = chain

    def wait_ge(self, sem, v):
        ev = _Ev(self._chain, "wait", f"wait {getattr(sem,'name','sem')}>={v}")
        ev.waits.append((id(sem), v, getattr(sem, "name", "sem")))
        self._rec.add(ev)
        self._eng.wait_ge(sem, v)

    def add_instruction(self, ins, **kw):
        ev = _Ev(self._chain, "op", "raw")
        self._rec.add(ev)
        return Shim(self._rec, self._eng.add_instruction(ins, **kw), ev)

    def __getattr__(self, opname):
        eng = self._eng
        fn = getattr(eng, opname)
        rec = self._rec
        chain = self._chain

        def call(*args, **kwargs):
            is_dma = opname in EngW._DMA
            ev = _Ev(chain, "dma_issue" if is_dma else "op", opname)
            aps = []
            for i, a in enumerate(args):
                if _is_ap(a):
                    aps.append((i, a))
            first_w = True
            for i, a in aps:
                f = _foot(rec, a)
                if first_w and i == 0:
                    ev.writes.append(f)
                    first_w = False
                else:
                    ev.reads.append(f)
            for k, a in kwargs.items():
                if _is_ap(a):
                    f = _foot(rec, a)
                    if k in ("accum_out", "out", "out_ap"):
                        ev.writes.append(f)
                    else:
                        ev.reads.append(f)
            if is_dma:
                rec.dma_n += 1
                done = _Ev(f"d{chain}{rec.dma_n}", "dma_done", opname)
                done.reads = ev.reads
                done.writes = ev.writes
                done.issue_ref = ev
                rec.add(ev)
                rec.add(done)
                instr = fn(*args, **kwargs)
                return Shim(rec, instr, done)
            rec.add(ev)
            instr = fn(*args, **kwargs)
            return Shim(rec, instr, ev)

        return call



def build_core(record=False, stop_phase=10**9, dump=None):
    nc = bass.Bass("TRN2", target_bir_lowering=False)
    rec = Recorder() if record else None

    x_h = nc.dram_tensor("x", [NB * T, D], F32, kind="ExternalInput").ap()
    out_h = nc.dram_tensor("out", [NB * T, D], F32, kind="ExternalOutput").ap()
    wnames = ["wq", "wk", "wv", "wo", "fq", "fk", "fv", "fo"]
    w_h = {n: nc.dram_tensor(n, [D, D], F32, kind="ExternalInput").ap() for n in wnames}
    col_h = {n: nc.dram_tensor(n, [128, NF], F32, kind="ExternalInput").ap()
             for n in ["fnw", "fnb", "fqb", "fkb"]}
    row_h = {n: nc.dram_tensor(n, [D], F32, kind="ExternalInput").ap()
             for n in ["fvb", "fob"]}

    es = ExitStack()
    sbuf = lambda name, shape, dt=BF16: es.enter_context(nc.sbuf_tensor(name, shape, dt))
    psum = lambda name, shape, dt=F32: es.enter_context(nc.psum_tensor(name, shape, dt))

    x_res = sbuf("x_res", [128, NT * 1024], F32)
    xnT = sbuf("xnT", [128, NF * T])
    yT = sbuf("yT", [128, NF * T])
    qT = sbuf("qT", [128, 2 * T])
    kT = sbuf("kT", [128, 2 * T])
    vaug = sbuf("vaug", [128, NT * 260])
    asq = sbuf("asq", [128, 2 * 1024])
    ypair = sbuf("ypair", [128, NT * 128])
    absq = ypair
    wtq = sbuf("wtq", [128, NF * 256])
    wtk = sbuf("wtk", [128, NF * 256])
    wtv = sbuf("wtv", [128, NF * 256])
    wto = sbuf("wto", [128, NF * 1024])
    xn2scr = sbuf("xn2scr", [128, 1024], F32)
    sqscr = sbuf("sqscr", [128, 1024], F32)
    xnbuf = [ypair[:, 0:1024], ypair[:, 1024:2048]]
    outscr = [sqscr, xn2scr]
    boundrow = sbuf("boundrow", [97, T])
    kmaxcol = sbuf("kmaxcol", [128, 2])
    stat = sbuf("stat", [128, 64], F32)
    statB = sbuf("statB", [128, 8], F32)
    zbuf = sbuf("zbuf", [128, 16], F32)
    rz = sbuf("rz", [128, 16], F32)
    vsum_sb = sbuf("vsum_sb", [65, 16])
    vsumT_sb = sbuf("vsumT_sb", [16, 65])
    cmask = sbuf("cmask", [128, 128])
    identb = sbuf("identb", [128, 128])
    ones16 = sbuf("ones16", [16, 128])
    negrow = sbuf("negrow", [97, 128])
    seedrow = sbuf("seedrow", [1, 455])
    zerorow = sbuf("zerorow", [1, 388])
    vbrow = sbuf("vbrow", [1, D])
    obrow = sbuf("obrow", [1, D])
    fqbc = sbuf("fqbc", [128, NF], F32)
    fkbc = sbuf("fkbc", [128, NF], F32)
    fnwc = sbuf("fnwc", [128, NF], F32)
    fnbc = sbuf("fnbc", [128, NF], F32)
    ceps = sbuf("ceps", [128, 1], F32)
    spcr = sbuf("spcr", [128, 2], F32)

    ps0 = psum("ps0", [128, 1024])
    ps1 = psum("ps1", [128, 1024])
    pv = [psum(f"pv{i}", [128, 512]) for i in range(3)]
    pt = psum("pt", [128, 1024], BF16)

    sX = es.enter_context(nc.semaphore("sX"))
    sXC = es.enter_context(nc.semaphore("sXC"))
    sRow = es.enter_context(nc.semaphore("sRow"))
    sW = es.enter_context(nc.semaphore("sW"))
    sWO = es.enter_context(nc.semaphore("sWO"))
    sOutP = [es.enter_context(nc.semaphore("sOutP0")),
             es.enter_context(nc.semaphore("sOutP1"))]
    sPE = es.enter_context(nc.semaphore("sPE"))
    sACT = es.enter_context(nc.semaphore("sACT"))
    sDVE = es.enter_context(nc.semaphore("sDVE"))
    sPool = es.enter_context(nc.semaphore("sPool"))

    cnt = {id(s): 0 for s in [sX, sXC, sRow, sW, sWO, sOutP[0], sOutP[1],
                              sPE, sACT, sDVE, sPool]}

    def inc(instr, s, n=1):
        instr.then_inc(s, n)
        cnt[id(s)] += n
        return cnt[id(s)]

    def cur(s):
        return cnt[id(s)]

    def wrap(eng, chain):
        return EngW(rec, eng, chain) if rec is not None else eng

    def tag(name):
        if rec is not None:
            rec.cur_phase = name

    def sp(v):
        v.tensor_copy(spcr[:, 0:1], spcr[:, 1:2])

    def w_slice_src(nm, g_i):
        return bass.AP(w_h[nm].tensor, g_i * 256,
                       [[D, 128], [128 * D, NF], [1, 256]])

    def w_full_src(nm):
        return bass.AP(w_h[nm].tensor, 0,
                       [[D, 128], [128 * D, NF], [1, 1024]])

    wtq_dst = bass.AP(wtq.tensor if hasattr(wtq, "tensor") else wtq, 0,
                      [[NF * 256, 128], [256, NF], [1, 256]])
    wtk_dst = bass.AP(wtk.tensor if hasattr(wtk, "tensor") else wtk, 0,
                      [[NF * 256, 128], [256, NF], [1, 256]])
    wtv_dst = bass.AP(wtv.tensor if hasattr(wtv, "tensor") else wtv, 0,
                      [[NF * 256, 128], [256, NF], [1, 256]])
    wto_dst = bass.AP(wto.tensor if hasattr(wto, "tensor") else wto, 0,
                      [[NF * 1024, 128], [1024, NF], [1, 1024]])

    def wt_group(g, stage, g_i):
        names = ("wq", "wk", "wv") if stage == 0 else ("fq", "fk", "fv")
        for nm, dst in zip(names, (wtq_dst, wtk_dst, wtv_dst)):
            inc(g.dma_start(dst, w_slice_src(nm, g_i)), sW, 16)

    def wt_out(g, stage):
        nm = "wo" if stage == 0 else "fo"
        inc(g.dma_start(wto_dst, w_full_src(nm)), sWO, 16)

    with nc.Block() as blk:
        @blk.sync
        def _(y_):
            y = wrap(y_, "Y")
            src = x_h[0:T, :].rearrange("(t p) f -> p t f", p=128)
            dst = bass.AP(x_res.tensor if hasattr(x_res, "tensor") else x_res,
                          0, [[NT * 1024, 128], [1024, NT], [1, 1024]])
            inc(y.dma_start(dst, src), sX, 16)
            for n, dstc in [("fqb", fqbc), ("fkb", fkbc),
                            ("fnw", fnwc), ("fnb", fnbc)]:
                inc(y.dma_start(dstc[:, :], col_h[n]), sXC, 16)

        @blk.gpsimd
        def _(g_):
            g = wrap(g_, "G")
            inc(g.dma_start(vbrow[0:1, :],
                            row_h["fvb"].rearrange("(a f) -> a f", a=1)), sRow, 16)
            inc(g.dma_start(obrow[0:1, :],
                            row_h["fob"].rearrange("(a f) -> a f", a=1)), sRow, 16)
            wt_group(g, 0, 0)
            wt_out(g, 0)
            g.memset(cmask[:, :], 1.0)
            g.affine_select(out=cmask[:, :], in_=cmask[:, :], compare_op=ALU.is_ge,
                            fill=0.0, base=0, pattern=[[1, 128]], channel_multiplier=-1)
            g.memset(identb[:, :], 0.0)
            g.affine_select(out=identb[:, :], in_=identb[:, :], compare_op=ALU.not_equal,
                            fill=1.0, base=0, pattern=[[-1, 128]], channel_multiplier=1)
            g.memset(ones16[:, :], 1.0)
            g.memset(negrow[:, :], -1.0)
            g.memset(ceps[:, :], LNEPS)
            g.memset(zerorow[:, :], 0.0)
            g.memset(seedrow[:, :], 0.0)
            last = None
            for j7 in range(7):
                last = g.memset(seedrow[0:1, j7 * 65 + 64: j7 * 65 + 65], EPS2)
            inc(last, sPool)

        nc.const_aps.register(F32, LNEPS, ceps[:, 0:1])

        @blk.scalar
        def _(s_):
            s = wrap(s_, "S")
            s.add_instruction(mybir.InstLoadActFuncSet(
                name=nc.get_next_instruction_name(),
                act_func_set_id=ACT_SET_LN_EXP, ins=[], outs=[]))

    def ln_stats(x_target=0, out_target=None):
        with nc.Block() as blk:
            dB = cur(sDVE)

            @blk.vector
            def _(v_):
                v = wrap(v_, "V")
                if x_target:
                    v.wait_ge(sX, x_target)
                if out_target is not None:
                    v.wait_ge(sOutP[0], out_target[0])
                    v.wait_ge(sOutP[1], out_target[1])
                for i in range(NT):
                    xt = x_res[:, i * 1024:(i + 1) * 1024]
                    v.scalar_tensor_tensor(sqscr[:, :], xt, 1.0, xt, ALU.mult,
                                           ALU.mult, accum_out=stat[:, 16 + i:17 + i])
                    v.reduce_sum(stat[:, i:i + 1], xt, axis=AX)
                sp(v)
                v.tensor_scalar(stat[:, 0:16], stat[:, 0:16], 1.0 / D, None, ALU.mult)
                sp(v)
                v.tensor_mul(stat[:, 48:64], stat[:, 0:16], stat[:, 0:16])
                sp(v)
                inc(v.scalar_tensor_tensor(stat[:, 16:32], stat[:, 16:32], 1.0 / D,
                                           stat[:, 48:64], ALU.mult, ALU.subtract), sDVE)

            @blk.scalar
            def _(s_):
                s = wrap(s_, "S")
                s.wait_ge(sDVE, cur(sDVE))
                s.activation(stat[:, 48:64], stat[:, 16:32], AF.Ln, bias=LNEPS, scale=1.0)
                inc(s.activation(stat[:, 32:48], stat[:, 48:64], AF.Exp,
                                 bias=0.0, scale=-0.5), sACT)
            _ = dB

    def ln1_apply(pool_memset_stage=None, pool_wait_pe=0):
        with nc.Block() as blk:
            pB, dB, aB = cur(sPE), cur(sDVE), cur(sACT)

            @blk.tensor
            def _(t_):
                t = wrap(t_, "T")
                t.wait_ge(sPool, 1)
                for i in range(NT):
                    t.wait_ge(sDVE, dB + 2 * i + 1)
                    mm = None
                    for j in range(NF):
                        mm = t.transpose(pt[:, j * 128:(j + 1) * 128],
                                         ypair[:, (i % 2) * 1024 + j * 128:
                                               (i % 2) * 1024 + (j + 1) * 128],
                                         identb[:, :])
                    inc(mm, sPE)

            @blk.vector
            def _(v_):
                v = wrap(v_, "V")
                v.wait_ge(sACT, aB)
                for i in range(NT):
                    xt = x_res[:, i * 1024:(i + 1) * 1024]
                    inc(v.tensor_scalar(xnbuf[i % 2], xt, stat[:, i:i + 1],
                                        stat[:, 32 + i:33 + i], ALU.subtract,
                                        ALU.mult), sDVE)
                    v.wait_ge(sPE, pB + i + 1)
                    dst = bass.AP(xnT.tensor if hasattr(xnT, "tensor") else xnT,
                                  i * 128, [[NF * T, 128], [T, NF], [1, 128]])
                    inc(v.tensor_copy(dst, pt[:, :]), sDVE)

            if pool_memset_stage is not None:
                @blk.gpsimd
                def _(g_):
                    g = wrap(g_, "G")
                    g.wait_ge(sPE, pool_wait_pe)
                    last = None
                    if pool_memset_stage == 0:
                        for i in range(NT):
                            for hh in range(4):
                                c0 = i * 260 + hh * 65 + 64
                                last = g.memset(vaug[:, c0:c0 + 1], 1.0)
                    else:
                        for i in range(NT):
                            for j2 in range(2):
                                c0 = i * 260 + j2 * 130 + 128
                                last = g.memset(vaug[:, c0:c0 + 1], 1.0)
                    inc(last, sPool)

    def ln2_apply(pool_wait_pe=0, prefetch=None):
        with nc.Block() as blk:
            pB, dB, aB = cur(sPE), cur(sDVE), cur(sACT)

            @blk.tensor
            def _(t_):
                t = wrap(t_, "T")
                for i in range(NT):
                    t.wait_ge(sDVE, dB + 4 * i + 2)
                    mm = None
                    for j in range(NF):
                        mm = t.transpose(pt[:, j * 128:(j + 1) * 128],
                                         ypair[:, (i % 2) * 1024 + j * 128:
                                               (i % 2) * 1024 + (j + 1) * 128],
                                         identb[:, :])
                    inc(mm, sPE)

            @blk.scalar
            def _(s_):
                s = wrap(s_, "S")
                for i in range(NT):
                    s.wait_ge(sDVE, dB + 4 * i + 1)
                    s.activation(statB[:, 2:3], statB[:, 1:2], AF.Ln,
                                 bias=LNEPS, scale=1.0)
                    inc(s.activation(statB[:, 3:4], statB[:, 2:3], AF.Exp,
                                     bias=0.0, scale=-0.5), sACT)

            @blk.vector
            def _(v_):
                v = wrap(v_, "V")
                v.wait_ge(sXC, 16 * 4)
                v.wait_ge(sACT, aB)
                for i in range(NT):
                    xt = x_res[:, i * 1024:(i + 1) * 1024]
                    v.tensor_scalar(xn2scr[:, :], xt, stat[:, i:i + 1],
                                    stat[:, 32 + i:33 + i], ALU.subtract, ALU.mult)
                    sp(v)
                    v.scalar_tensor_tensor(sqscr[:, :], xn2scr[:, :], 1.0,
                                           xn2scr[:, :], ALU.mult, ALU.mult,
                                           accum_out=statB[:, 4:5])
                    v.reduce_sum(statB[:, 5:6], xn2scr[:, :], axis=AX)
                    sp(v)
                    v.tensor_scalar(statB[:, 0:1], statB[:, 5:6], 1.0 / D, None, ALU.mult)
                    sp(v)
                    v.tensor_mul(statB[:, 6:7], statB[:, 0:1], statB[:, 0:1])
                    sp(v)
                    inc(v.scalar_tensor_tensor(statB[:, 1:2], statB[:, 4:5], 1.0 / D,
                                               statB[:, 6:7], ALU.mult, ALU.subtract),
                        sDVE)
                    v.wait_ge(sACT, aB + i + 1)
                    inc(v.tensor_scalar(xnbuf[i % 2], xn2scr[:, :], statB[:, 0:1],
                                        statB[:, 3:4], ALU.subtract, ALU.mult), sDVE)
                    inc(v.tensor_add(xt, xt, xn2scr[:, :]), sDVE)
                    v.wait_ge(sPE, pB + i + 1)
                    mm = None
                    for j in range(NF):
                        mm = v.tensor_scalar(xnT[:, j * T + i * 128:j * T + (i + 1) * 128],
                                             pt[:, j * 128:(j + 1) * 128],
                                             fnwc[:, j:j + 1], fnbc[:, j:j + 1],
                                             ALU.mult, ALU.add)
                    inc(mm, sDVE)

            @blk.gpsimd
            def _(g_):
                g = wrap(g_, "G")
                g.wait_ge(sPE, pool_wait_pe)
                last = None
                for i in range(NT):
                    for j2 in range(2):
                        c0 = i * 260 + j2 * 130 + 128
                        last = g.memset(vaug[:, c0:c0 + 1], 1.0)
                inc(last, sPool)
                if prefetch is not None:
                    prefetch(g)

    def qkv_phase(stage, g_i, wt_target, pool_target, prefetch=None):
        with nc.Block() as blk:
            pB, dB = cur(sPE), cur(sDVE)
            chains = [(proj, d, c) for proj in (0, 1) for d in range(2)
                      for c in range(4)]
            nqk = len(chains)

            def nK(c):
                return 8 if stage == 0 else [8, 4, 2, 1][c]

            @blk.tensor
            def _(t_):
                t = wrap(t_, "T")
                t.wait_ge(sW, wt_target)
                if stage == 1:
                    t.wait_ge(sRow, 32)
                t.wait_ge(sPool, pool_target)
                t.wait_ge(sDVE, dB)
                for idx, (proj, d, c) in enumerate(chains):
                    if idx >= 3:
                        t.wait_ge(sDVE, dB + idx - 2)
                    w = wtq if proj == 0 else wtk
                    dest = pv[idx % 3]
                    mm = None
                    for kk in range(nK(c)):
                        mm = t.matmul(dest[:, 0:512],
                                 w[:, kk * 256 + d * 128: kk * 256 + (d + 1) * 128],
                                 xnT[:, kk * T + c * 512: kk * T + (c + 1) * 512],
                                 start=(kk == 0), stop=(kk == nK(c) - 1))
                    inc(mm, sPE)
                for i in range(NT):
                    idx = nqk + i
                    t.wait_ge(sDVE, dB + idx - 2)
                    dest = pv[idx % 3]
                    c = i // 4
                    if stage == 1:
                        t.matmul(dest[:, 0:256], ones16[0:1, 0:128],
                                 vbrow[0:1, g_i * 256:(g_i + 1) * 256],
                                 start=True, stop=False)
                    mm = None
                    for kk in range(nK(c)):
                        mm = t.matmul(dest[:, 0:256],
                                 xnT[:, kk * T + i * 128: kk * T + (i + 1) * 128],
                                 wtv[:, kk * 256:(kk + 1) * 256],
                                 start=(stage == 0 and kk == 0),
                                 stop=(kk == nK(c) - 1))
                    inc(mm, sPE)

            @blk.vector
            def _(v_):
                v = wrap(v_, "V")
                if stage == 1:
                    v.wait_ge(sXC, 16 * 4)
                for idx, (proj, d, c) in enumerate(chains):
                    v.wait_ge(sPE, pB + idx + 1)
                    src = pv[idx % 3][:, 0:512]
                    dstT = qT if proj == 0 else kT
                    dcol = d * T + c * 512
                    if stage == 0:
                        if proj == 0:
                            mm = v.tensor_scalar(dstT[:, dcol:dcol + 512], src,
                                                 0.125, None, ALU.mult)
                        else:
                            mm = v.tensor_copy(dstT[:, dcol:dcol + 512], src)
                    else:
                        bc = (fqbc if proj == 0 else fkbc)[:, g_i * 2 + d:g_i * 2 + d + 1]
                        mm = v.tensor_scalar(dstT[:, dcol:dcol + 512], src,
                                             bc, None, ALU.add)
                    inc(mm, sDVE)
                for i in range(NT):
                    idx = nqk + i
                    v.wait_ge(sPE, pB + idx + 1)
                    src = pv[idx % 3][:, 0:256]
                    if stage == 0:
                        dst = bass.AP(vaug.tensor if hasattr(vaug, "tensor") else vaug,
                                      i * 260, [[NT * 260, 128], [65, 4], [1, 64]])
                    else:
                        dst = bass.AP(vaug.tensor if hasattr(vaug, "tensor") else vaug,
                                      i * 260, [[NT * 260, 128], [130, 2], [1, 128]])
                    inc(v.tensor_copy(dst, src), sDVE)

            if prefetch is not None:
                @blk.gpsimd
                def _(g_):
                    g = wrap(g_, "G")
                    prefetch(g)

    def att_A(g_i, prefetch=None, prefetch_wait_pe=None):
        with nc.Block() as blk:
            pB, dB, aB = cur(sPE), cur(sDVE), cur(sACT)
            strips = []
            for k in range(NT):
                for s in range(k * 128 // 1024, 2):
                    tq0 = max(k * 128, s * 1024)
                    strips.append((k, s, tq0, (s + 1) * 1024 - tq0, s == k // 8))
            nstr = len(strips)

            heads = []
            pe_c, act_c, dve_c = pB, aB, dB
            for j in range(4):
                h = {"score": {}, "av": {}, "evac": {}, "mask": {}}
                pe_c += 1
                h["score"][strips[0][0:2]] = pe_c
                pe_c += 1
                h["seed"] = pe_c
                for n, (k, s, tq0, w, dg) in enumerate(strips):
                    if n + 1 < nstr:
                        pe_c += 1
                        h["score"][strips[n + 1][0:2]] = pe_c
                    pe_c += 1
                    h["av"][(k, s)] = pe_c
                for (k, s, tq0, w, dg) in strips:
                    act_c += 1
                    h["evac"][(k, s)] = act_c
                    if dg:
                        dve_c += 1
                        h["mask"][k] = dve_c
                dve_c += 1
                h["vsum_evac"] = dve_c
                pe_c += 1
                h["vsumT_tr"] = pe_c
                dve_c += 1
                h["vsumT_evac"] = dve_c
                pe_c += 1
                h["headdone"] = pe_c
                dve_c += 1
                h["division"] = dve_c
                if j % 2 == 1:
                    h["pairtr"] = []
                    for r in range(2):
                        pe_c += 1
                        dve_c += 1
                        h["pairtr"].append((pe_c, dve_c))
                heads.append(h)

            def bank_of(i):
                return (0, i) if i < 7 else ((1, i - 7) if i < 14 else (2, i - 14))

            @blk.tensor
            def _(t_):
                t = wrap(t_, "T")
                scnt = 0
                for j in range(4):
                    h = heads[j]
                    d, rr = j // 2, (j % 2) * 64
                    if j == 0:
                        t.wait_ge(sDVE, dB)

                    def scoreA(n, scnt_n):
                        k, s, tq0, w, dg = strips[n]
                        psP = (ps0, ps1)[scnt_n % 2]
                        if scnt_n >= 2:
                            t.wait_ge(sACT, aB + scnt_n - 1)
                        r0 = tq0 - s * 1024
                        mm = None
                        for c0, c1 in ((r0, 512), (max(r0, 512), 1024)):
                            if c0 < c1:
                                mm = t.matmul(psP[:, c0:c1],
                                         kT[rr:rr + 64,
                                            d * T + k * 128: d * T + (k + 1) * 128],
                                         qT[rr:rr + 64,
                                            d * T + s * 1024 + c0: d * T + s * 1024 + c1],
                                         start=True, stop=True)
                        inc(mm, sPE)
                        assert cur(sPE) == h["score"][(k, s)]

                    scoreA(0, scnt)
                    if j > 0:
                        t.wait_ge(sDVE, heads[j - 1]["division"])
                    mm = None
                    for pvx, w in ((pv[0], 455), (pv[1], 455), (pv[2], 130)):
                        mm = t.matmul(pvx[:, 0:w], ones16[0:1, 0:128],
                                      seedrow[0:1, 0:w], start=True, stop=False)
                    inc(mm, sPE)
                    assert cur(sPE) == h["seed"]
                    for n, (k, s, tq0, w, dg) in enumerate(strips):
                        if n + 1 < nstr:
                            scoreA(n + 1, scnt + 1)
                        par = scnt % 2
                        if dg:
                            t.wait_ge(sDVE, h["mask"][k])
                        else:
                            t.wait_ge(sACT, h["evac"][(k, s)])
                        vsl = vaug[:, k * 260 + j * 65: k * 260 + j * 65 + 65]
                        i_lo = max(k, 8 * s)
                        mm = None
                        for i in range(i_lo, 8 * s + 8):
                            bk, off = bank_of(i)
                            mm = t.matmul(pv[bk][:, off * 65: off * 65 + 65],
                                     asq[:, par * 1024 + i * 128 - tq0:
                                         par * 1024 + (i + 1) * 128 - tq0],
                                     vsl, start=False, stop=False)
                        if dg:
                            bk, off = bank_of(k)
                            t.matmul(pv[bk][:, off * 65: off * 65 + 65],
                                     cmask[:, :], vsl, start=False, stop=False)
                            mm = t.matmul(pv[2][0:65, 200 + k: 201 + k], vsl,
                                     cmask[:, 127:128], start=False, stop=False)
                        inc(mm, sPE)
                        assert cur(sPE) == h["av"][(k, s)]
                        scnt += 1
                    t.wait_ge(sDVE, h["vsum_evac"])
                    t.wait_ge(sACT, aB + nstr * (j + 1))
                    inc(t.transpose(pt[0:16, 0:65], vsum_sb[:, :],
                                    identb[0:65, 0:65]), sPE)
                    assert cur(sPE) == h["vsumT_tr"]
                    t.wait_ge(sDVE, h["vsumT_evac"])
                    mm = None
                    for i in range(1, NT):
                        bk, off = bank_of(i)
                        mm = t.matmul(pv[bk][:, off * 65: off * 65 + 65],
                                      ones16[0:i, :], vsumT_sb[0:i, 0:65],
                                      start=False, stop=True)
                    inc(mm, sPE)
                    assert cur(sPE) == h["headdone"]
                    if j % 2 == 1:
                        t.wait_ge(sDVE, h["division"])
                        for r in range(2):
                            if r == 1:
                                t.wait_ge(sDVE, h["pairtr"][0][1])
                            mm = None
                            for i8 in range(8):
                                it = r * 8 + i8
                                mm = t.transpose(pt[:, i8 * 128:(i8 + 1) * 128],
                                                 ypair[:, it * 128:(it + 1) * 128],
                                                 identb[:, :])
                            inc(mm, sPE)
                            assert cur(sPE) == h["pairtr"][r][0]

            @blk.scalar
            def _(s_c):
                s_ = wrap(s_c, "S")
                scnt = 0
                for j in range(4):
                    h = heads[j]
                    for (k, s, tq0, w, dg) in strips:
                        s_.wait_ge(sPE, h["score"][(k, s)])
                        if scnt >= 2:
                            gk, gs = (strips * 4)[scnt - 2][0:2]
                            jj = (scnt - 2) // nstr
                            s_.wait_ge(sPE, heads[jj]["av"][(gk, gs)])
                        par = scnt % 2
                        r0 = tq0 - s * 1024
                        psP = (ps0, ps1)[scnt % 2]
                        mm = s_.activation(asq[:, par * 1024: par * 1024 + w],
                                           psP[:, r0:1024], AF.Square,
                                           bias=1.0, scale=1.0)
                        inc(mm, sACT)
                        assert cur(sACT) == h["evac"][(k, s)]
                        scnt += 1

            @blk.vector
            def _(v_):
                v = wrap(v_, "V")
                scnt = 0
                for j in range(4):
                    h = heads[j]
                    for (k, s, tq0, w, dg) in strips:
                        if dg:
                            v.wait_ge(sACT, h["evac"][(k, s)])
                            par = scnt % 2
                            inc(v.tensor_mul(asq[:, par * 1024: par * 1024 + 128],
                                             asq[:, par * 1024: par * 1024 + 128],
                                             cmask[:, :]), sDVE)
                            assert cur(sDVE) == h["mask"][k]
                        scnt += 1
                    v.wait_ge(sPE, h["av"][(15, 1)])
                    inc(v.tensor_copy(vsum_sb[:, :], pv[2][0:65, 200:216]), sDVE)
                    assert cur(sDVE) == h["vsum_evac"]
                    v.wait_ge(sPE, h["vsumT_tr"])
                    inc(v.tensor_copy(vsumT_sb[:, :], pt[0:16, 0:65]), sDVE)
                    assert cur(sDVE) == h["vsumT_evac"]
                    v.wait_ge(sPE, h["headdone"])
                    v.tensor_copy(zbuf[:, 0:7],
                                  bass.AP(pv[0].tensor if hasattr(pv[0], "tensor") else pv[0],
                                          64, [[512, 128], [65, 7], [1, 1]]))
                    v.tensor_copy(zbuf[:, 7:14],
                                  bass.AP(pv[1].tensor if hasattr(pv[1], "tensor") else pv[1],
                                          64, [[512, 128], [65, 7], [1, 1]]))
                    v.tensor_copy(zbuf[:, 14:16],
                                  bass.AP(pv[2].tensor if hasattr(pv[2], "tensor") else pv[2],
                                          64, [[512, 128], [65, 2], [1, 1]]))
                    sp(v)
                    v.reciprocal(rz[:, :], zbuf[:, :])
                    sp(v)
                    p_off = (j % 2) * 64
                    mm = None
                    for i in range(NT):
                        bk, off = bank_of(i)
                        mm = v.tensor_scalar(ypair[:, i * 128 + p_off: i * 128 + p_off + 64],
                                             pv[bk][:, off * 65: off * 65 + 64],
                                             rz[:, i:i + 1], None, ALU.mult)
                    inc(mm, sDVE)
                    assert cur(sDVE) == h["division"]
                    if j % 2 == 1:
                        pr = g_i * 2 + j // 2
                        for r in range(2):
                            v.wait_ge(sPE, h["pairtr"][r][0])
                            mm = v.tensor_copy(
                                yT[:, pr * T + r * 1024: pr * T + (r + 1) * 1024],
                                pt[:, :])
                            inc(mm, sDVE)
                            assert cur(sDVE) == h["pairtr"][r][1]

            if prefetch is not None:
                @blk.gpsimd
                def _(g_):
                    g = wrap(g_, "G")
                    if prefetch_wait_pe:
                        g.wait_ge(sPE, prefetch_wait_pe)
                    prefetch(g)

    def wo_phase():
        with nc.Block() as blk:
            pB, dB = cur(sPE), cur(sDVE)
            wt_target = cur(sWO)

            @blk.tensor
            def _(t_):
                t = wrap(t_, "T")
                t.wait_ge(sWO, wt_target)
                t.wait_ge(sDVE, dB)
                idx = 0
                for i in range(NT):
                    for ch in range(2):
                        if idx >= 3:
                            t.wait_ge(sDVE, dB + idx - 2)
                        dest = pv[idx % 3]
                        mm = None
                        for j in range(NF):
                            mm = t.matmul(dest[:, 0:512],
                                     yT[:, j * T + i * 128: j * T + (i + 1) * 128],
                                     wto[:, j * 1024 + ch * 512: j * 1024 + (ch + 1) * 512],
                                     start=(j == 0), stop=(j == NF - 1))
                        inc(mm, sPE)
                        idx += 1

            @blk.vector
            def _(v_):
                v = wrap(v_, "V")
                idx = 0
                for i in range(NT):
                    for ch in range(2):
                        v.wait_ge(sPE, pB + idx + 1)
                        xsl = x_res[:, i * 1024 + ch * 512: i * 1024 + (ch + 1) * 512]
                        inc(v.tensor_add(xsl, xsl, pv[idx % 3][:, 0:512]), sDVE)
                        idx += 1

    def bounds_phase(g_i):
        with nc.Block() as blk:
            pB, dB, aB = cur(sPE), cur(sDVE), cur(sACT)

            @blk.tensor
            def _(t_):
                t = wrap(t_, "T")
                t.wait_ge(sACT, aB)
                t.wait_ge(sDVE, dB + 1)
                mm = None
                for c2 in range(2):
                    mm = t.matmul(ps0[0:1, c2 * 512:(c2 + 1) * 512], kmaxcol[:, 0:1],
                             absq[:, c2 * 512:(c2 + 1) * 512], start=True, stop=True)
                inc(mm, sPE)
                for c2 in range(2):
                    mm = t.matmul(ps1[0:1, c2 * 512:(c2 + 1) * 512], kmaxcol[:, 0:1],
                             absq[:, 1024 + c2 * 512: 1024 + (c2 + 1) * 512],
                             start=True, stop=True)
                inc(mm, sPE)
                t.wait_ge(sDVE, dB + 4)
                for c2 in range(2):
                    mm = t.matmul(ps0[0:1, c2 * 512:(c2 + 1) * 512], kmaxcol[:, 1:2],
                             absq[:, c2 * 512:(c2 + 1) * 512], start=True, stop=True)
                inc(mm, sPE)
                for c2 in range(2):
                    mm = t.matmul(ps1[0:1, c2 * 512:(c2 + 1) * 512], kmaxcol[:, 1:2],
                             absq[:, 1024 + c2 * 512: 1024 + (c2 + 1) * 512],
                             start=True, stop=True)
                inc(mm, sPE)

            @blk.vector
            def _(v_):
                v = wrap(v_, "V")
                v.wait_ge(sPE, pB)
                for j in range(2):
                    v.reduce_max(kmaxcol[:, j:j + 1], kT[:, j * T:(j + 1) * T],
                                 axis=AX, apply_absolute_value=True)
                inc(v.scalar_tensor_tensor(absq[:, 0:2048], qT[:, 0:T], -1.0,
                                           qT[:, 0:T], ALU.mult, ALU.max), sDVE)
                v.wait_ge(sPE, pB + 2)
                v.tensor_copy(boundrow[32:33, 0:1024], ps0[0:1, 0:1024])
                v.tensor_copy(boundrow[32:33, 1024:2048], ps1[0:1, 0:1024])
                inc(v.tensor_copy(boundrow[0:1, 0:1024], ps0[0:1, 0:1024]), sDVE)
                inc(v.tensor_copy(boundrow[0:1, 1024:2048], ps1[0:1, 0:1024]), sDVE)
                inc(v.scalar_tensor_tensor(absq[:, 0:2048], qT[:, T:2 * T], -1.0,
                                           qT[:, T:2 * T], ALU.mult, ALU.max), sDVE)
                v.wait_ge(sPE, pB + 4)
                v.tensor_copy(boundrow[96:97, 0:1024], ps0[0:1, 0:1024])
                v.tensor_copy(boundrow[96:97, 1024:2048], ps1[0:1, 0:1024])
                inc(v.tensor_copy(boundrow[64:65, 0:1024], ps0[0:1, 0:1024]), sDVE)
                inc(v.tensor_copy(boundrow[64:65, 1024:2048], ps1[0:1, 0:1024]), sDVE)
            return cur(sDVE)

    def att_B(g_i, bounds_done, prefetch=None, prefetch_wait_pe=None):
        with nc.Block() as blk:
            pB, dB, aB = cur(sPE), cur(sDVE), cur(sACT)
            units = [(j, c2) for j in range(2) for c2 in range(2)]
            plans = []
            pe_c, act_c, dve_c = pB, aB, dB
            trs = {}
            for (j, c2) in units:
                h = {}
                h["score"] = {}
                h["av"] = {}
                h["evac"] = {}
                pe_c += 1
                h["score"][0] = pe_c
                pe_c += 1
                h["seed"] = pe_c
                for k in range(NT):
                    if k + 1 < NT:
                        pe_c += 1
                        h["score"][k + 1] = pe_c
                    pe_c += 1
                    h["av"][k] = pe_c
                for k in range(NT):
                    act_c += 1
                    h["evac"][k] = act_c
                dve_c += 1
                h["division"] = dve_c
                plans.append(h)
                if c2 == 1:
                    lst = []
                    for r in range(2):
                        pe_c += 1
                        dve_c += 1
                        lst.append((pe_c, dve_c))
                    trs[j] = lst

            @blk.tensor
            def _(t_):
                t = wrap(t_, "T")
                for u, (j, c2) in enumerate(units):
                    h = plans[u]
                    if u == 0:
                        t.wait_ge(sDVE, bounds_done)
                    jrow = j * 64

                    def scoreB(k):
                        psP = (ps0, ps1)[k % 2]
                        gidx = u * NT + k
                        if gidx >= 2:
                            t.wait_ge(sACT, aB + gidx - 1)
                        for half in range(2):
                            r = jrow + 32 * half
                            t.matmul(psP[:, half * 512:(half + 1) * 512],
                                     negrow[r:r + 1, 0:128],
                                     boundrow[r:r + 1,
                                              c2 * 1024 + half * 512:
                                              c2 * 1024 + (half + 1) * 512],
                                     start=True, stop=False,
                                     tile_position=(r, 0),
                                     skip_group_check=True)
                        mm = None
                        for half in range(2):
                            mm = t.matmul(psP[:, half * 512:(half + 1) * 512],
                                     kT[:, j * T + k * 128: j * T + (k + 1) * 128],
                                     qT[:, j * T + c2 * 1024 + half * 512:
                                        j * T + c2 * 1024 + (half + 1) * 512],
                                     start=False, stop=True,
                                     skip_group_check=True)
                        inc(mm, sPE)
                        assert cur(sPE) == h["score"][k]

                    scoreB(0)
                    if u > 0:
                        t.wait_ge(sDVE, plans[u - 1]["division"])
                    mm = None
                    for pvx, w in ((pv[0], 387), (pv[1], 387), (pv[2], 258)):
                        mm = t.matmul(pvx[:, 0:w], ones16[0:1, 0:128],
                                      zerorow[0:1, 0:w], start=True, stop=False)
                    inc(mm, sPE)
                    assert cur(sPE) == h["seed"]
                    for k in range(NT):
                        if k + 1 < NT:
                            scoreB(k + 1)
                        t.wait_ge(sACT, h["evac"][k])
                        par = k % 2
                        vsl = vaug[:, k * 260 + j * 130: k * 260 + j * 130 + 129]
                        mm = None
                        for i in range(8):
                            bk, off = (0, i) if i < 3 else ((1, i - 3) if i < 6 else (2, i - 6))
                            mm = t.matmul(pv[bk][:, off * 129: off * 129 + 129],
                                          asq[:, par * 1024 + i * 128: par * 1024 + (i + 1) * 128],
                                          vsl, start=False, stop=(k == NT - 1))
                        inc(mm, sPE)
                        assert cur(sPE) == h["av"][k]
                    if c2 == 1:
                        t.wait_ge(sDVE, h["division"])
                        for r in range(2):
                            if r == 1:
                                t.wait_ge(sDVE, trs[j][0][1])
                            mm = None
                            for i8 in range(8):
                                it = r * 8 + i8
                                mm = t.transpose(pt[:, i8 * 128:(i8 + 1) * 128],
                                                 ypair[:, it * 128:(it + 1) * 128],
                                                 identb[:, :])
                            inc(mm, sPE)
                            assert cur(sPE) == trs[j][r][0]

            @blk.scalar
            def _(s_):
                s = wrap(s_, "S")
                for u, (j, c2) in enumerate(units):
                    h = plans[u]
                    for k in range(NT):
                        par = k % 2
                        s.wait_ge(sPE, h["score"][k])
                        if k >= 2:
                            s.wait_ge(sPE, h["av"][k - 2])
                        elif u > 0:
                            s.wait_ge(sPE, plans[u - 1]["av"][NT - 2 + k])
                        mm = s.activation(asq[:, par * 1024: par * 1024 + 1024],
                                          (ps0, ps1)[k % 2][:, 0:1024], AF.Exp,
                                          bias=0.0, scale=1.0 / 32.0)
                        inc(mm, sACT)
                        assert cur(sACT) == h["evac"][k]

            @blk.vector
            def _(v_):
                v = wrap(v_, "V")
                for u, (j, c2) in enumerate(units):
                    h = plans[u]
                    v.wait_ge(sPE, h["av"][NT - 1])
                    v.tensor_copy(zbuf[:, 0:3],
                                  bass.AP(pv[0].tensor if hasattr(pv[0], "tensor") else pv[0],
                                          128, [[512, 128], [129, 3], [1, 1]]))
                    v.tensor_copy(zbuf[:, 3:6],
                                  bass.AP(pv[1].tensor if hasattr(pv[1], "tensor") else pv[1],
                                          128, [[512, 128], [129, 3], [1, 1]]))
                    v.tensor_copy(zbuf[:, 6:8],
                                  bass.AP(pv[2].tensor if hasattr(pv[2], "tensor") else pv[2],
                                          128, [[512, 128], [129, 2], [1, 1]]))
                    sp(v)
                    v.reciprocal(rz[:, 0:8], zbuf[:, 0:8])
                    sp(v)
                    mm = None
                    for i in range(8):
                        it = c2 * 8 + i
                        bk, off = (0, i) if i < 3 else ((1, i - 3) if i < 6 else (2, i - 6))
                        mm = v.tensor_scalar(
                            ypair[:, it * 128:(it + 1) * 128],
                            pv[bk][:, off * 129: off * 129 + 128],
                            rz[:, i:i + 1], None, ALU.mult)
                    inc(mm, sDVE)
                    assert cur(sDVE) == h["division"]
                    if c2 == 1:
                        pr = g_i * 2 + j
                        for r in range(2):
                            v.wait_ge(sPE, trs[j][r][0])
                            mm = v.tensor_copy(
                                yT[:, pr * T + r * 1024: pr * T + (r + 1) * 1024],
                                pt[:, :])
                            inc(mm, sDVE)
                            assert cur(sDVE) == trs[j][r][1]

            if prefetch is not None:
                @blk.gpsimd
                def _(g_):
                    g = wrap(g_, "G")
                    if prefetch_wait_pe:
                        g.wait_ge(sPE, prefetch_wait_pe)
                    prefetch(g)

    def out_phase(b, xload_slot=None):
        echunks = {0: [(0, 512), (512, 1024)], 1: [(0, 512)], 2: [(0, 256)],
                   3: [(0, 128)]}
        with nc.Block() as blk:
            pB, dB = cur(sPE), cur(sDVE)
            oB = [cur(sOutP[0]), cur(sOutP[1])]
            wt_target = cur(sWO)
            tasks = [(e, i) for e in range(4) for i in range(4 * e, 4 * e + 4)]
            nchunks = sum(len(echunks[e]) for e, i in tasks)

            task_of_chunk = []
            for ti, (e, i) in enumerate(tasks):
                for _ in echunks[e]:
                    task_of_chunk.append(ti)

            @blk.tensor
            def _(t_):
                t = wrap(t_, "T")
                t.wait_ge(sWO, wt_target)
                t.wait_ge(sRow, 32)
                t.wait_ge(sDVE, dB)
                idx = 0
                for e, i in tasks:
                    nK = [8, 4, 2, 1][e]
                    for (o0, o1) in echunks[e]:
                        if idx >= 3:
                            t.wait_ge(sDVE, dB + task_of_chunk[idx - 3] + 1)
                        dest = pv[idx % 3]
                        t.matmul(dest[:, 0:o1 - o0], ones16[0:1, 0:128],
                                 obrow[0:1, o0:o1], start=True, stop=False)
                        mm = None
                        for kk in range(nK):
                            mm = t.matmul(dest[:, 0:o1 - o0],
                                     yT[:, kk * T + i * 128: kk * T + (i + 1) * 128],
                                     wto[:, kk * 1024 + o0: kk * 1024 + o1],
                                     start=False, stop=(kk == nK - 1))
                        inc(mm, sPE)
                        idx += 1

            @blk.vector
            def _(v_):
                v = wrap(v_, "V")
                idx = 0
                for ti, (e, i) in enumerate(tasks):
                    par = ti % 2
                    m_e = 1024 >> e
                    if ti >= 2:
                        v.wait_ge(sOutP[par], oB[par] + (ti // 2) * 16)
                    xt = x_res[:, i * 1024:(i + 1) * 1024]
                    last = None
                    for (o0, o1) in echunks[e]:
                        v.wait_ge(sPE, pB + idx + 1)
                        last = v.tensor_add(outscr[par][:, o0:o1], xt[:, o0:o1],
                                            pv[idx % 3][:, 0:o1 - o0])
                        idx += 1
                    if m_e < 1024:
                        last = v.tensor_copy(outscr[par][:, m_e:1024], xt[:, m_e:1024])
                    inc(last, sDVE)

            @blk.sync
            def _(y_):
                y = wrap(y_, "Y")
                for ti, (e, i) in enumerate(tasks):
                    par = ti % 2
                    y.wait_ge(sDVE, dB + ti + 1)
                    inc(y.dma_start(out_h[b * T + i * 128: b * T + (i + 1) * 128, :],
                                    outscr[par][:, :]), sOutP[par], 16)
                if b == 0:
                    y.wait_ge(sDVE, dB + NT)
                    srcx = x_h[T:2 * T, :].rearrange("(t p) f -> p t f", p=128)
                    dstx = bass.AP(x_res.tensor if hasattr(x_res, "tensor") else x_res,
                                   0, [[NT * 1024, 128], [1024, NT], [1, 1024]])
                    xload_slot[0] = inc(y.dma_start(dstx, srcx), sX, 16)
                else:
                    y.wait_ge(sOutP[0], cur(sOutP[0]))
                    y.wait_ge(sOutP[1], cur(sOutP[1]))

            @blk.gpsimd
            def _(g_):
                g = wrap(g_, "G")
                if b == 0:
                    g.wait_ge(sPE, pB + nchunks)
                    wt_out(g, 0)

    phase_n = [0]

    def ph():
        phase_n[0] += 1
        return phase_n[0] <= stop_phase

    xload_slot = [16]
    out_after_b0 = None
    for b in range(NB):
        xt_target = xload_slot[0]
        if ph():
            tag(f"b{b}ln_stats1")
            ln_stats(x_target=xt_target, out_target=(None if b == 0 else out_after_b0))
        if ph():
            tag(f"b{b}ln1")
            ln1_apply(pool_memset_stage=0, pool_wait_pe=cur(sPE))
        poolA = cur(sPool)
        for g_i in range(4):
            if ph():
                tag(f"b{b}qkvA{g_i}")
                qkv_phase(0, g_i, wt_target=cur(sW), pool_target=poolA)
            if ph():
                tag(f"b{b}attA{g_i}")
                if g_i < 3:
                    att_A(g_i, prefetch=(lambda g, gn=g_i: wt_group(g, 0, gn + 1)),
                          prefetch_wait_pe=cur(sPE))
                else:
                    att_A(g_i, prefetch=(lambda g: wt_group(g, 1, 0)),
                          prefetch_wait_pe=cur(sPE))
        if ph():
            tag(f"b{b}wo")
            wo_phase()
        if ph():
            tag(f"b{b}ln_stats2")
            ln_stats()
        if ph():
            tag(f"b{b}ln2")
            ln2_apply(pool_wait_pe=cur(sPE),
                      prefetch=(lambda g: wt_out(g, 1)))
        poolB = cur(sPool)
        for g_i in range(4):
            if ph():
                tag(f"b{b}qkvB{g_i}")
                qkv_phase(1, g_i, wt_target=cur(sW), pool_target=poolB)
            if ph():
                tag(f"b{b}bounds{g_i}")
                bdone = bounds_phase(g_i)
            if ph():
                tag(f"b{b}attB{g_i}")
                if g_i < 3:
                    att_B(g_i, bdone, prefetch=(lambda g, gn=g_i: wt_group(g, 1, gn + 1)),
                          prefetch_wait_pe=cur(sPE))
                elif b == 0:
                    att_B(g_i, bdone, prefetch=(lambda g: wt_group(g, 0, 0)),
                          prefetch_wait_pe=cur(sPE))
                else:
                    att_B(g_i, bdone)
        if ph():
            tag(f"b{b}out")
            out_phase(b, xload_slot)
        if b == 0:
            out_after_b0 = (cur(sOutP[0]), cur(sOutP[1]))

    with nc.Block() as blk:
        if phase_n[0] > stop_phase:
            @blk.tensor
            def _(t_):
                t = wrap(t_, "T")
                t.wait_ge(sDVE, cur(sDVE))
                t.wait_ge(sACT, cur(sACT))
            @blk.vector
            def _(v_):
                v = wrap(v_, "V")
                v.wait_ge(sPE, cur(sPE))
                v.wait_ge(sPool, cur(sPool))
            @blk.scalar
            def _(s_):
                s = wrap(s_, "S")
                s.wait_ge(sPE, cur(sPE))
            @blk.gpsimd
            def _(g_):
                g = wrap(g_, "G")
                g.wait_ge(sPE, cur(sPE))
            dump_map = {"x_res": (x_res, NT * 1024), "xnT": (xnT, NF * T),
                        "yT": (yT, NF * T), "qT": (qT, 2 * T), "kT": (kT, 2 * T),
                        "vaug": (vaug, NT * 260), "asq": (asq, 2048),
                        "ypair": (ypair, NT * 128), "boundrow": (boundrow, T),
                        "stat": (stat, 64), "wtq": (wtq, NF * 256),
                        "wtv": (wtv, NF * 256), "wto": (wto, NF * 1024)}
            dt_, dw_ = dump_map[dump] if dump else (x_res, NT * 1024)

            @blk.gpsimd
            def _(g_):
                g = wrap(g_, "G")
                g.wait_ge(sX, cur(sX))
                g.wait_ge(sXC, cur(sXC))
                g.wait_ge(sRow, cur(sRow))
                g.wait_ge(sW, cur(sW))
                g.wait_ge(sWO, cur(sWO))
                g.wait_ge(sPE, cur(sPE))
                g.wait_ge(sDVE, cur(sDVE))
                g.wait_ge(sACT, cur(sACT))
                nt_ = (dw_ + 1023) // 1024
                for i in range(NB * NT):
                    ii = i % nt_
                    w = min(1024, dw_ - ii * 1024)
                    inc(g.dma_start(out_h[i * 128:(i + 1) * 128, 0:w],
                                    dt_[:, ii * 1024: ii * 1024 + w]),
                        sOutP[i % 2], 16)
                g.wait_ge(sOutP[0], cur(sOutP[0]))
                g.wait_ge(sOutP[1], cur(sOutP[1]))
        else:
            @blk.sync
            def _(y_):
                y = wrap(y_, "Y")
                y.wait_ge(sOutP[0], cur(sOutP[0]))
                y.wait_ge(sOutP[1], cur(sOutP[1]))

    es.close()
    return (nc, rec) if record else nc



def verify(rec):
    chains = sorted(rec.chains.keys())
    cidx = {c: i for i, c in enumerate(chains)}
    nch = len(chains)
    heads = {c: 0 for c in chains}
    semval = {}
    sem_prefix = {}
    fire_order = []

    races = []

    def join(a, b):
        return [max(x, y) for x, y in zip(a, b)]

    prev_clock = {c: [0] * nch for c in chains}

    progressed = True
    total = sum(len(v) for v in rec.chains.values())
    fired = 0
    while fired < total:
        progressed = False
        for c in chains:
            ch = rec.chains[c]
            while heads[c] < len(ch):
                ev = ch[heads[c]]
                if ev.kind == "dma_done" and ev.issue_ref.fire < 0:
                    break
                ok = True
                for sid, v, nm in ev.waits:
                    if semval.get(sid, 0) < v:
                        ok = False
                        break
                if not ok:
                    break
                clk = list(prev_clock[c])
                for sid, v, nm in ev.waits:
                    lst = sem_prefix.get(sid, [])
                    lo, hi = 0, len(lst)
                    while lo < hi:
                        mid = (lo + hi) // 2
                        if lst[mid][0] >= v:
                            hi = mid
                        else:
                            lo = mid + 1
                    if lo < len(lst):
                        clk = join(clk, lst[lo][1])
                    elif lst:
                        clk = join(clk, lst[-1][1])
                clk[cidx[c]] = heads[c] + 1
                ev.clock = clk
                ev.fire = len(fire_order)
                fire_order.append(ev)
                prev_clock[c] = clk
                for sid, n, nm in ev.incs:
                    semval[sid] = semval.get(sid, 0) + n
                    lst = sem_prefix.setdefault(sid, [])
                    pc = join(lst[-1][1], clk) if lst else clk
                    lst.append((semval[sid], pc))
                heads[c] += 1
                fired += 1
                progressed = True
        if not progressed:
            msgs = []
            for c in chains:
                if heads[c] < len(rec.chains[c]):
                    ev = rec.chains[c][heads[c]]
                    msgs.append(f"  chain {c} stuck at #{heads[c]} {ev.label} "
                                f"waits={[(nm, v, semval.get(sid, 0)) for sid, v, nm in ev.waits]}")
            raise RuntimeError("DEADLOCK:\n" + "\n".join(msgs))

    def hb(a, b):
        return b.clock[cidx[a.chain]] >= a.idx + 1

    sem_issues = []
    for c in chains:
        for ev in rec.chains[c]:
            for sid, v, nm in ev.waits:
                incs = rec.sem_incs.get(sid, [])
                if len({ie.chain for ie, n, iss in incs}) <= 1:
                    continue
                run = 0
                k = len(incs)
                for idx2, (ie, n, iss) in enumerate(incs):
                    run += n
                    if run >= v:
                        k = idx2 + 1
                        break
                for ie, n, iss in incs[k:]:
                    if not hb(ev, iss):
                        sem_issues.append((nm, v, ev.chain, ev.idx, ev.label,
                                           iss.chain, iss.idx, iss.label))
    for s in sem_issues[:40]:
        print("SEM-ORDER", s)
    if sem_issues:
        races.append(("SEM-ORDER-COUNT", len(sem_issues), "", 0, "", "", 0, ""))

    state = {}

    def cells_of(foot):
        name, pr, ivals, is_psum, esz = foot
        pgs = set(p // 64 for p in range(pr[0], pr[1], 64)) | {(pr[1] - 1) // 64}
        out = []
        for lo, hi in ivals:
            for cell in range(lo // 128, (hi - 1) // 128 + 1):
                for pg in pgs:
                    out.append(((name, pg, cell), lo, hi))
        return out

    def overlap(iv1, iv2):
        return iv1[0] < iv2[1] and iv2[0] < iv1[1]

    bank_state = {}

    def access(ev, foot, is_write):
        name, pr, ivals, is_psum, esz = foot
        if is_psum:
            banks = set()
            for lo, hi in ivals:
                b0 = (lo * esz) // 2048
                b1 = ((hi - 1) * esz) // 2048
                banks.update(range(b0, b1 + 1))
            for bk in banks:
                st = bank_state.setdefault((name, bk), [None, {}])
                if ev.chain == "T" and is_write:
                    for c, other in st[1].items():
                        if not hb(other, ev):
                            races.append(("PSUM-BANK", (name, bk), other.chain,
                                          other.idx, other.label, ev.chain,
                                          ev.idx, ev.label))
                    st[0] = ev
                elif ev.chain in ("V", "S"):
                    if st[0] is not None and not hb(st[0], ev):
                        races.append(("PSUM-BANK", (name, bk), st[0].chain,
                                      st[0].idx, st[0].label, ev.chain,
                                      ev.idx, ev.label))
                    st[1][ev.chain] = ev
        for key, lo, hi in cells_of(foot):
            st = state.setdefault(key, {})
            for oc, rec2 in st.items():
                if oc == ev.chain:
                    continue
                ow_iv, ow_ev, or_iv, or_ev = rec2
                if is_write:
                    if ow_ev is not None and any(overlap((lo, hi), i) for i in ow_iv):
                        _chk(ow_ev, ev, races, key, "WW")
                    if or_ev is not None and any(overlap((lo, hi), i) for i in or_iv):
                        _chk(or_ev, ev, races, key, "WR->W")
                else:
                    if ow_ev is not None and any(overlap((lo, hi), i) for i in ow_iv):
                        _chk(ow_ev, ev, races, key, "W->R")
            mine = st.setdefault(ev.chain, [[], None, [], None])
            if is_write:
                mine[0].append((lo, hi))
                mine[1] = ev
            else:
                mine[2].append((lo, hi))
                mine[3] = ev

    def _chk(a, b, races, key, kind):
        aa = a
        bb = b
        if a.kind == "dma_done":
            aa = a
        if b.kind == "dma_done" and b.issue_ref is not None:
            bb = b.issue_ref
        if not hb(aa, bb if bb.clock is not None else b):
            races.append((kind, key, a.chain, a.idx, a.label, b.chain, b.idx, b.label))

    for ev in fire_order:
        if ev.kind == "dma_issue":
            continue
        for f in ev.reads:
            access(ev, f, False)
        for f in ev.writes:
            access(ev, f, True)

    def foot_overlap(fa, fb):
        if fa[0] != fb[0]:
            return False
        if fa[1][1] <= fb[1][0] or fb[1][1] <= fa[1][0]:
            return False
        for a in fa[2]:
            for b in fb[2]:
                if a[0] < b[1] and b[0] < a[1]:
                    return True
        return False

    for chain in ("V",):
        cV = cidx[chain]
        evs = [e for e in rec.chains.get(chain, []) if e.kind == "op"]
        other = [c for c in chains if c != chain and not c.startswith("d")]
        for i in range(len(evs) - 1):
            a, b = evs[i], evs[i + 1]
            hit = False
            for fw in a.writes:
                for fr in b.reads + b.writes:
                    if foot_overlap(fw, fr):
                        hit = True
            if not hit:
                continue
            proven = False
            for c in other:
                lst = rec.chains[c]
                lo, hi = 0, len(lst)
                while lo < hi:
                    mid = (lo + hi) // 2
                    if lst[mid].clock is not None and lst[mid].clock[cV] >= a.idx + 1:
                        hi = mid
                    else:
                        lo = mid + 1
                if lo < len(lst) and b.clock[cidx[c]] >= lo + 1:
                    proven = True
                    break
            if not proven:
                for fw in a.writes:
                    for fr in b.reads + b.writes:
                        if foot_overlap(fw, fr):
                            races.append(("DVE-ADJ", fw[0], a.chain, a.idx,
                                          a.label, b.chain, b.idx, b.label))
    return races



_CACHE = {}


def kernel(**inputs):
    e_cnt = np.asarray(inputs["L0_ffn_router_prob"]).shape[2]
    assert e_cnt == 4, f"kernel built for 4 experts, got {e_cnt}"
    x = np.ascontiguousarray(np.asarray(inputs["L0_attn_hidden_states"],
                                        dtype=np.float32))
    wt = {
        "wq": "L0_attn_Wq", "wk": "L0_attn_Wk", "wv": "L0_attn_Wv",
        "wo": "L0_attn_Wo", "fq": "L0_ffn_q_w", "fk": "L0_ffn_k_w",
        "fv": "L0_ffn_v_w", "fo": "L0_ffn_o_w",
    }
    common = {}
    for k, v in wt.items():
        w = np.asarray(inputs[v], dtype=np.float32)
        common[k] = np.ascontiguousarray(w.T)
    for k, v in [("fnw", "L0_ffn_norm_w"), ("fnb", "L0_ffn_norm_b"),
                 ("fqb", "L0_ffn_q_b"), ("fkb", "L0_ffn_k_b")]:
        a = np.asarray(inputs[v], dtype=np.float32)
        common[k] = np.ascontiguousarray(a.reshape(NF, 128).T)
    for k, v in [("fvb", "L0_ffn_v_b"), ("fob", "L0_ffn_o_b")]:
        common[k] = np.ascontiguousarray(np.asarray(inputs[v], dtype=np.float32))
    stop_phase = _CACHE.get("stop_phase", 10**9)
    dump = _CACHE.get("dump", None)
    key = ("nc", stop_phase, dump)
    if key not in _CACHE:
        _CACHE[key] = build_core(stop_phase=stop_phase, dump=dump)
    nc = _CACHE[key]
    in_maps = []
    for c in range(8):
        m = dict(common)
        m["x"] = np.ascontiguousarray(x[2 * c:2 * c + 2].reshape(NB * T, D))
        in_maps.append(m)
    res = run_bass_kernel_spmd(nc, in_maps, core_ids=list(range(8)))
    out = np.concatenate([r["out"].reshape(NB, T, D) for r in res.results], axis=0)
    return out.astype(np.float32)


if __name__ == "__main__":
    if "--check" in sys.argv:
        nc, rec = build_core(record=True)
        print("built; chains:", {c: len(v) for c, v in rec.chains.items()})
        races = verify(rec)
        print(f"verify: {len(races)} potential races")
        for r in races[:60]:
            print("  RACE", r)
    else:
        nc = build_core()
        print("built OK; instructions:", nc.n_instructions())
